# version 03
# baseline (speedup 1.0000x reference)
import contextlib
import numpy as np
import ml_dtypes
import concourse.bass as bass
import concourse.mybir as mybir
from concourse.bass_utils import run_bass_kernel_spmd
from concourse.bass_types import AP as _AP


def _ap(a):
    if a is not None and not isinstance(a, _AP) and hasattr(a, "ap"):
        return a.ap()
    return a

F32 = mybir.dt.float32
BF16 = mybir.dt.bfloat16
AF = mybir.ActivationFunctionType
ALU = mybir.AluOpType
AX = mybir.AxisListType
RND = 12582912.0
BF = ml_dtypes.bfloat16

S, D = 8192, 2048
NCORES = 8
TS = S // NCORES
NT = TS // 128
NTF = S // 128

NH = 16
HPC = NH // NCORES
Q_LORA, KV_LORA = 1536, 512
NOPE, ROPE_D, VH = 128, 64, 128
QH = NOPE + ROPE_D
SCALE = QH ** -0.5

BK, BV, DK, DV, KCONV = 16, 32, 128, 128, 4
KPC = BK // NCORES
VPC = BV // NCORES

PI_HI = float(np.float32(6.2831855))
PI_LO = float(np.float64(2 * np.pi) - np.float64(np.float32(6.2831855)))
INV2PI = float(np.float32(1.0 / (2 * np.pi)))
PI_CLAMP = 3.14159255
RG = [list(range(NCORES))]


SERIAL_PHASES = set()
STRICT_GPSIMD = False
STRICT_DMA = False


class B:

    ENGS = ("sync", "tensor", "vector", "scalar", "gpsimd")

    def __init__(self, nc):
        self.nc = nc
        self.seq = []
        self.k = 0
        self.phase_marks = {}

    def mark_phase(self, name, start, end):
        self.phase_marks[name] = (start, end)

    def op(self, eng, method, *args, inc=1, **kwargs):
        args = tuple(_ap(a) for a in args)
        kwargs = {k_: (_ap(v) if k_ in ("out", "in_", "bias", "scale",
                                        "accum_out") else v)
                  for k_, v in kwargs.items()}
        self.seq.append((eng, self.k, method, args, kwargs, inc))
        self.k += inc

    def dma(self, out, in_, eng="sync"):
        self.op(eng, "dma_start", inc=16, out=out, in_=in_)

    def ts(self, out, in0, s1, s2=None, op0=ALU.mult, op1=ALU.bypass,
           eng="vector"):
        if s2 is None:
            self.op(eng, "tensor_scalar", out, in0, s1, None, op0=op0)
        else:
            self.op(eng, "tensor_scalar", out, in0, s1, s2, op0=op0, op1=op1)

    def tt(self, out, in0, in1, alu, eng="vector"):
        self.op(eng, "tensor_tensor", out, in0, in1, op=alu)

    def stt(self, out, in0, scalar, in1, op0, op1, eng="vector"):
        self.op(eng, "scalar_tensor_tensor", out, in0, scalar, in1,
                op0=op0, op1=op1)

    def cp(self, out, in_, eng="vector"):
        self.op(eng, "tensor_copy", out, in_)

    def act(self, out, in_, func, bias=0.0, scale=1.0, accum_out=None,
            eng="scalar"):
        kw = dict(bias=bias, scale=scale)
        if accum_out is not None:
            kw["accum_out"] = accum_out
        self.op(eng, "activation", out, in_, func, **kw)

    def mm(self, psum, lhsT, rhs, start, stop, eng="tensor"):
        self.op(eng, "matmul", psum, lhsT, rhs, start=start, stop=stop)

    def mm_acc(self, psum, ops):
        n = len(ops)
        for i, (l, r) in enumerate(ops):
            self.mm(psum, l, r, i == 0, i == n - 1)

    def tp(self, psum128, src, dst, ident, evac_eng="vector"):
        self.op("tensor", "transpose", psum128, src, ident)
        self.cp(dst, psum128, eng=evac_eng)

    def bcast_rows(self, dst, src_row, nrows=128, eng="gpsimd"):
        self.dma(dst[0:1, :], src_row[0:1, :], eng=eng)
        n = 1
        while n < nrows:
            m = min(n, nrows - n)
            self.dma(dst[n:n + m, :], dst[0:m, :], eng=eng)
            n += m

    def barrier(self):
        self.seq.append(("sync", self.k, "__barrier__", (), {}, 0))

    def coll(self, kind, alu, in_t, out_t):
        self.op("gpsimd", "collective_compute", kind, alu,
                replica_groups=RG, ins=[in_t.ap().opt()],
                outs=[out_t.ap().opt()])

    _WRITE_POS = {
        "matmul": ((0,), True), "transpose": ((0,), False),
        "tensor_scalar": ((0,), False), "tensor_tensor": ((0,), False),
        "scalar_tensor_tensor": ((0,), False), "tensor_copy": ((0,), False),
        "tensor_tensor_scan": ((0,), False), "memset": ((0,), False),
        "reduce_max": ((0,), False), "reduce_sum": ((0,), False),
        "reciprocal": ((0,), False), "iota": ((0,), False),
        "affine_select": ((0,), False), "activation": ((0,), False),
        "tensor_single_scalar": ((0,), False),
    }

    @staticmethod
    def _span(ap):
        if not isinstance(ap, _AP):
            return None
        off = ap.offset
        if not isinstance(off, int):
            raise RuntimeError("dynamic AP offsets unsupported in tracker")
        ext = 1
        for s_, c_ in ap.ap:
            ext += (c_ - 1) * abs(s_)
        return (ap.tensor.name, off, off + ext)

    def _rw(self, meth, args, kw):
        wr, rd = [], []
        if meth == "dma_start":
            wr.append(self._span(kw["out"]))
            rd.append(self._span(kw["in_"]))
        elif meth == "dma_start_transpose":
            wr.append(self._span(args[0]))
            rd.append(self._span(args[1]))
        elif meth == "collective_compute":
            for a in kw.get("ins", []):
                rd.append(self._span(a))
                wr.append(self._span(a))
            for a in kw.get("outs", []):
                wr.append(self._span(a))
        else:
            wpos, rmw = self._WRITE_POS.get(meth, ((), False))
            for i, a in enumerate(args):
                sp = self._span(a)
                if sp is None:
                    continue
                if i in wpos:
                    wr.append(sp)
                    if rmw:
                        rd.append(sp)
                else:
                    rd.append(sp)
            for k_, v in kw.items():
                sp = self._span(v)
                if sp is None:
                    continue
                if k_ == "accum_out":
                    wr.append(sp)
                else:
                    rd.append(sp)
        return [s for s in wr if s], [s for s in rd if s]

    def check_hb(self):
        n = len(self.seq)
        is_dma = [m in ("dma_start", "dma_start_transpose")
                  for (_, _, m, _, _, _) in self.seq]
        is_marker = [m == "__barrier__" for (_, _, m, _, _, _) in self.seq]
        dom = [("d_" + e) if is_dma[i] else ("c_" + e)
               for i, (e, _, m, _, _, _) in enumerate(self.seq)]
        deps, marked, cnt, _run = self._deps_pass()
        seqno = [0] * n
        runall = {}
        for i in range(n):
            if is_marker[i]:
                continue
            runall[dom[i]] = runall.get(dom[i], 0) + 1
            seqno[i] = runall[dom[i]]
        cnt2seq = {}
        for i in range(n):
            if marked[i]:
                cnt2seq.setdefault(dom[i], {})[cnt[i]] = seqno[i]
        clock = [None] * n
        lastate = {}
        for i in range(n):
            if is_marker[i]:
                continue
            eng = self.seq[i][0]
            c = dict(lastate.get(eng + "_clk", {}))
            for j in deps[i]:
                d_ = dom[j]
                s_ = cnt2seq.get(d_, {}).get(cnt[j], 0)
                if s_ > c.get(d_, 0):
                    c[d_] = s_
                cj = clock[j]
                if cj:
                    for d2, s2 in cj.items():
                        if s2 > c.get(d2, 0):
                            c[d2] = s2
            clock[i] = c
            lastate[eng + "_clk"] = c
        acc = {}
        bad = []
        for i in range(n):
            if is_marker[i]:
                continue
            (_, _, meth, args, kw, _) = self.seq[i]
            wr, rd = self._rw(meth, args, kw)
            ci = clock[i]
            eng = self.seq[i][0]
            def ordered(j):
                d_ = dom[j]
                return ci.get(d_, 0) >= seqno[j]
            for (t, lo, hi) in rd + wr:
                for (alo, ahi, aw, aj) in acc.get(t, ()):
                    conflict = (alo < hi and lo < ahi and
                                (aw or (t, lo, hi) in [x for x in wr]))
                    if conflict and not ordered(aj):
                        bad.append((aj, i, t))
                        if len(bad) > 50:
                            return bad
            for (t, lo, hi) in rd:
                acc.setdefault(t, []).append((lo, hi, False, i))
            for (t, lo, hi) in wr:
                acc.setdefault(t, []).append((lo, hi, True, i))
        return bad

    def _meta(self):
        n = len(self.seq)
        is_dma = [m in ("dma_start", "dma_start_transpose")
                  for (_, _, m, _, _, _) in self.seq]
        is_marker = [m == "__barrier__" for (_, _, m, _, _, _) in self.seq]
        dom = [("d_" + e) if is_dma[i] else ("c_" + e)
               for i, (e, _, m, _, _, _) in enumerate(self.seq)]
        return n, is_dma, is_marker, dom

    def _deps_pass(self):
        if getattr(self, "_deps_cache", None) is not None:
            return self._deps_cache
        n, is_dma, is_marker, dom = self._meta()
        acc = {}
        floor_dep = {}
        deps = [set() for _ in range(n)]
        last_in_dom = {}
        barrier_set = set()
        last_gp = {}
        last_q = {}
        serial_idx = set()
        for name_ in SERIAL_PHASES:
            parts = name_.split(":")
            lo, hi = self.phase_marks.get(parts[0], (0, 0))
            if len(parts) == 3:
                f0, f1 = float(parts[1]), float(parts[2])
                lo, hi = (lo + int((hi - lo) * f0)), (lo + int((hi - lo) * f1))
            serial_idx |= set(range(lo, hi))
        prev_real = None
        for i, (eng, _, meth, args, kw, inc) in enumerate(self.seq):
            if meth == "__barrier__":
                barrier_set = set(last_in_dom.values())
                continue
            wr, rd = self._rw(meth, args, kw)
            deps[i] |= barrier_set
            if i in serial_idx and prev_real is not None:
                deps[i].add(prev_real)
            prev_real = i
            if STRICT_GPSIMD and eng == "gpsimd":
                lp = last_gp.get("x")
                if lp is not None:
                    deps[i].add(lp)
                last_gp["x"] = i
            if STRICT_DMA and meth in ("dma_start", "dma_start_transpose"):
                lp = last_q.get(eng)
                if lp is not None:
                    deps[i].add(lp)
                last_q[eng] = i
            if meth == "collective_compute":
                deps[i] |= set(last_in_dom.values())
            for (t, lo, hi) in rd:
                for (alo, ahi, aw, aj) in acc.get(t, ()):
                    if aw and alo < hi and lo < ahi:
                        deps[i].add(aj)
                deps[i] |= floor_dep.get(t, set())
            for (t, lo, hi) in wr:
                for (alo, ahi, aw, aj) in acc.get(t, ()):
                    if alo < hi and lo < ahi:
                        deps[i].add(aj)
                deps[i] |= floor_dep.get(t, set())
            for (t, lo, hi) in rd:
                acc.setdefault(t, []).append((lo, hi, False, i))
            for (t, lo, hi) in wr:
                acc.setdefault(t, []).append((lo, hi, True, i))
            for (t, lo, hi) in wr + rd:
                lst = acc.get(t)
                if lst and len(lst) > 64:
                    drop, keep = lst[:32], lst[32:]
                    acc[t] = keep
                    fd = floor_dep.setdefault(t, set())
                    fd |= {j for (_, _, _, j) in drop}
                    if len(fd) > 16:
                        best = {}
                        for j in fd:
                            d_ = dom[j]
                            if d_ not in best or j > best[d_]:
                                best[d_] = j
                        floor_dep[t] = set(best.values())
            deps[i].discard(i)
            last_in_dom[dom[i]] = i
        marked = [False] * n
        for i in range(n):
            if is_dma[i]:
                marked[i] = True
            for j in deps[i]:
                marked[j] = True
        for i, (eng, _, meth, args, kw, inc) in enumerate(self.seq):
            if is_dma[i]:
                sp = self._span(kw["out"]) if meth == "dma_start" else \
                    self._span(args[0])
                if sp and sp[0] == "x_out":
                    marked[i] = True
        cnt = [0] * n
        run = {}
        for i in range(n):
            if marked[i]:
                run[dom[i]] = run.get(dom[i], 0) + 1
                cnt[i] = run[dom[i]]
        self._deps_cache = (deps, marked, cnt, dict(run))
        return self._deps_cache

    def emit(self, block, sems):
        n, is_dma, is_marker, dom = self._meta()
        deps, marked, cnt, run = self._deps_pass()
        per = {e: [] for e in self.ENGS}
        for i, (eng, _, meth, args, kw, inc) in enumerate(self.seq):
            if not is_marker[i]:
                per[eng].append(i)
        final_counts = dict(run)

        def unit(d):
            return 16 if d.startswith("d_") else 1

        for name in self.ENGS:
            idxs = per[name]
            if not idxs and name != "sync":
                continue
            dec = getattr(block, name)

            def body(e, idxs=idxs, name=name):
                vclock = {}
                for i in idxs:
                    (eng, _, meth, args, kw, inc) = self.seq[i]
                    want = {}
                    for j in sorted(deps[i]):
                        d_ = dom[j]
                        c_ = cnt[j]
                        if c_ > want.get(d_, 0):
                            want[d_] = c_
                    for d_, c_ in want.items():
                        if vclock.get(d_, 0) >= c_:
                            continue
                        e.wait_ge(sems[d_], c_ * unit(d_))
                        vclock[d_] = c_
                    try:
                        ins = getattr(e, meth)(*args, **kw)
                    except Exception as ex:
                        raise RuntimeError(
                            f"emit failed: {name}.{meth}") from ex
                    if marked[i]:
                        d_ = dom[i]
                        ins.then_inc(sems[d_], unit(d_))
                if name == "sync":
                    for d_, c_ in final_counts.items():
                        e.wait_ge(sems[d_], c_ * unit(d_))
            dec(body)


def emit_identity(b, ID):
    b.op("vector", "memset", ID[:], 1.0)
    b.op("gpsimd", "affine_select", ID[:], ID[:], pattern=[[-1, 128]],
         compare_op=ALU.is_ge, fill=0.0, base=0, channel_multiplier=1)
    b.op("gpsimd", "affine_select", ID[:], ID[:], pattern=[[1, 128]],
         compare_op=ALU.is_ge, fill=0.0, base=0, channel_multiplier=-1)


def emit_ln_tm(b, x_ap, out_ap, cols, D_, eps, sub_mean=True, sq_scratch=None):
    s1, s2, mcol, rstd = cols[:, 0:1], cols[:, 1:2], cols[:, 2:3], cols[:, 3:4]
    sq = sq_scratch if sq_scratch is not None else out_ap
    b.act(sq, x_ap, AF.Square, accum_out=s2)
    if sub_mean:
        b.op("vector", "reduce_sum", s1, x_ap, axis=AX.X)
        b.ts(mcol, s1, 1.0 / D_)
        b.ts(s2, s2, 1.0 / D_)
        b.tt(s1, mcol, mcol, ALU.mult)
        b.tt(s2, s2, s1, ALU.subtract)
    else:
        b.ts(s2, s2, 1.0 / D_)
    b.ts(s2, s2, eps, op0=ALU.add)
    b.act(s1, s2, AF.Sqrt)
    b.op("vector", "reciprocal", rstd, s1)
    if sub_mean:
        b.ts(out_ap, x_ap, mcol, rstd, op0=ALU.subtract, op1=ALU.mult)
    else:
        b.ts(out_ap, x_ap, rstd, op0=ALU.mult)


def _emit_sincos(b, src, K1, dst_sin, dst_cos, TMPA):
    for (dst, phase) in ((dst_sin, 0.0), (dst_cos, np.pi / 2)):
        if phase:
            b.ts(TMPA, src, float(np.float32(phase)), op0=ALU.add)
            s_ = TMPA
        else:
            s_ = src
        b.ts(K1, s_, INV2PI, RND, op0=ALU.mult, op1=ALU.add)
        b.ts(K1, K1, RND, op0=ALU.subtract)
        b.stt(dst, K1, -PI_HI, s_, op0=ALU.mult, op1=ALU.add)
        b.stt(dst, K1, -PI_LO, dst, op0=ALU.mult, op1=ALU.add)
        b.ts(dst, dst, PI_CLAMP, -PI_CLAMP, op0=ALU.min, op1=ALU.max)
        b.act(dst, dst, AF.Sin)


def emit_rope_apply(b, x1, x2, COS, SIN, T1, T2, T3):
    b.tt(T1, x1, COS, ALU.mult)
    b.tt(T2, x2, SIN, ALU.mult)
    b.tt(T1, T1, T2, ALU.subtract)
    b.tt(T2, x2, COS, ALU.mult)
    b.tt(T3, x1, SIN, ALU.mult)
    b.tt(x2, T2, T3, ALU.add)
    b.cp(x1, T1)


def emit_phase1(nc, b, G, ins, agql_i, agkv_i):
    with contextlib.ExitStack() as ctx:
        E = ctx.enter_context
        XT = E(nc.sbuf_tensor("XT1", [128, D], F32))
        LN1B = E(nc.sbuf_tensor("LN1B", [128, D], BF16))
        LN1F = E(nc.sbuf_tensor("LN1F", [128, D], BF16))
        WQA = E(nc.sbuf_tensor("WQA", [128, 16 * Q_LORA], BF16))
        WKVA = E(nc.sbuf_tensor("WKVA", [128, 16 * 576], BF16))
        QL = E(nc.sbuf_tensor("QL", [128, Q_LORA], F32))
        QLB = E(nc.sbuf_tensor("QLB", [128, Q_LORA], BF16))
        QLFO = E(nc.sbuf_tensor("QLFO", [128, 128], BF16))
        KV = E(nc.sbuf_tensor("KV1", [128, 640], F32))
        KVB = E(nc.sbuf_tensor("KVB1", [128, 640], BF16))
        KVFO = E(nc.sbuf_tensor("KVFO", [128, 128], BF16))
        QLNW = E(nc.sbuf_tensor("QLNW", [128, Q_LORA], F32))
        KVLNW = E(nc.sbuf_tensor("KVLNW", [128, KV_LORA], F32))
        TMP = E(nc.sbuf_tensor("TMP1", [128, D], F32))
        POSC = E(nc.sbuf_tensor("POSC", [128, 1], F32))
        IVB = E(nc.sbuf_tensor("IVB", [128, 32], F32))
        RSC = E(nc.sbuf_tensor("RSC", [128, 8 * 32], F32))
        PS0 = E(nc.psum_tensor("PS01", [128, 512], F32))
        PS2 = E(nc.psum_tensor("PS21", [128, 512], F32))
        PS1B = E(nc.psum_tensor("PS1B1", [128, 512], BF16))
        COLS = G["COLS"]

        F_, K1 = RSC[:, 0:32], RSC[:, 32:64]
        COS, SIN = RSC[:, 64:96], RSC[:, 96:128]
        T1, T2, T3 = RSC[:, 128:160], RSC[:, 160:192], RSC[:, 192:224]
        TMPA = RSC[:, 224:256]

        for kc in range(16):
            b.dma(WQA[:, kc * Q_LORA:(kc + 1) * Q_LORA],
                  ins["wqaT"][kc * 128:(kc + 1) * 128, :], eng="gpsimd")
            b.dma(WKVA[:, kc * 576:(kc + 1) * 576],
                  ins["wkvaT"][kc * 128:(kc + 1) * 128, :], eng="gpsimd")
        b.dma(TMP[0:1, 0:Q_LORA], ins["qlnw"][0:1, :], eng="gpsimd")
        b.bcast_rows(QLNW, TMP[0:1, 0:Q_LORA])
        b.dma(TMP[0:1, 0:KV_LORA], ins["kvlnw"][0:1, :], eng="gpsimd")
        b.bcast_rows(KVLNW, TMP[0:1, 0:KV_LORA])
        b.dma(TMP[0:1, 0:32], ins["invf"][0:1, :], eng="gpsimd")
        b.bcast_rows(IVB, TMP[0:1, 0:32])
        b.op("vector", "memset", KVB[:, 576:640], 0.0)

        for i in range(NT):
            rows = slice(i * 128, (i + 1) * 128)
            b.dma(XT[:, :], ins["x_in"][rows, :])
            b.dma(POSC[:, :], ins["pos"][rows, :])
            b.ts(F_, IVB, POSC[:, 0:1], op0=ALU.mult)
            _emit_sincos(b, F_, K1, SIN, COS, TMPA)
            emit_ln_tm(b, XT[:, :], LN1B, COLS, D, 1e-5, sub_mean=True,
                       sq_scratch=TMP)
            for kc in range(16):
                b.tp(PS1B[:, 0:128], LN1B[:, kc * 128:(kc + 1) * 128],
                     LN1F[:, kc * 128:(kc + 1) * 128], G["ID"][:])
            for nb in range(3):
                b.mm_acc(PS0[:, 0:512],
                         [(LN1F[:, kc * 128:(kc + 1) * 128],
                           WQA[:, kc * Q_LORA + nb * 512:
                               kc * Q_LORA + (nb + 1) * 512])
                          for kc in range(16)])
                b.cp(QL[:, nb * 512:(nb + 1) * 512], PS0[:, 0:512])
            b.act(TMP[:, 0:Q_LORA], QL[:, :], AF.Square,
                  accum_out=COLS[:, 1:2])
            b.ts(COLS[:, 1:2], COLS[:, 1:2], 1.0 / Q_LORA)
            b.ts(COLS[:, 1:2], COLS[:, 1:2], 1e-6, op0=ALU.add)
            b.act(COLS[:, 0:1], COLS[:, 1:2], AF.Sqrt)
            b.op("vector", "reciprocal", COLS[:, 3:4], COLS[:, 0:1])
            b.ts(QL[:, :], QL[:, :], COLS[:, 3:4], op0=ALU.mult)
            b.tt(QLB[:, :], QL[:, :], QLNW[:, :], ALU.mult)
            for kc in range(12):
                b.tp(PS1B[:, 0:128], QLB[:, kc * 128:(kc + 1) * 128],
                     QLFO[:, :], G["ID"][:])
                b.dma(agql_i[kc * 128:(kc + 1) * 128, rows], QLFO[:, :])
            b.mm_acc(PS0[:, 0:512],
                     [(LN1F[:, kc * 128:(kc + 1) * 128],
                       WKVA[:, kc * 576: kc * 576 + 512])
                      for kc in range(16)])
            b.cp(KV[:, 0:512], PS0[:, 0:512])
            b.mm_acc(PS2[:, 0:64],
                     [(LN1F[:, kc * 128:(kc + 1) * 128],
                       WKVA[:, kc * 576 + 512: kc * 576 + 576])
                      for kc in range(16)])
            b.cp(KV[:, 512:576], PS2[:, 0:64])
            b.act(TMP[:, 0:512], KV[:, 0:512], AF.Square,
                  accum_out=COLS[:, 1:2])
            b.ts(COLS[:, 1:2], COLS[:, 1:2], 1.0 / KV_LORA)
            b.ts(COLS[:, 1:2], COLS[:, 1:2], 1e-6, op0=ALU.add)
            b.act(COLS[:, 0:1], COLS[:, 1:2], AF.Sqrt)
            b.op("vector", "reciprocal", COLS[:, 3:4], COLS[:, 0:1])
            b.ts(KV[:, 0:512], KV[:, 0:512], COLS[:, 3:4], op0=ALU.mult)
            b.tt(KVB[:, 0:512], KV[:, 0:512], KVLNW[:, :], ALU.mult)
            emit_rope_apply(b, KV[:, 512:544], KV[:, 544:576], COS, SIN,
                            T1, T2, T3)
            b.cp(KVB[:, 512:576], KV[:, 512:576])
            for kc in range(5):
                b.tp(PS1B[:, 0:128], KVB[:, kc * 128:(kc + 1) * 128],
                     KVFO[:, :], G["ID"][:])
                b.dma(agkv_i[kc * 128:(kc + 1) * 128, rows], KVFO[:, :])


def emit_phase2(nc, b, G, ins, agql_o, agkv_o, xhf_d):
    with contextlib.ExitStack() as ctx:
        E = ctx.enter_context
        QN0 = E(nc.sbuf_tensor("QN0", [128, S], BF16))
        QN1 = E(nc.sbuf_tensor("QN1", [128, S], BF16))
        QX1 = E(nc.sbuf_tensor("QX1", [64, S], BF16))
        QX2 = E(nc.sbuf_tensor("QX2", [64, S], BF16))
        KPX1 = E(nc.sbuf_tensor("KPX1", [64, S], BF16))
        KPX2 = E(nc.sbuf_tensor("KPX2", [64, S], BF16))
        PB = E(nc.sbuf_tensor("PB", [128, S], BF16))
        PTB = E(nc.sbuf_tensor("PTB", [128, 128], BF16))
        MX = E(nc.sbuf_tensor("MX", [128, 16], F32))
        LC = E(nc.sbuf_tensor("LC", [128, 16], F32))
        MASK = E(nc.sbuf_tensor("MASK", [128, 4 * 512], BF16))
        XH = E(nc.sbuf_tensor("XH", [128, VH], F32))
        XHB = E(nc.sbuf_tensor("XHB", [128, VH], BF16))
        PSS = E(nc.psum_tensor("PSS", [128, 512], F32))
        PSK = E(nc.psum_tensor("PSK2", [128, 512], F32))
        PSV = E(nc.psum_tensor("PSV2", [128, 512], F32))
        PSO = E(nc.psum_tensor("PSO2", [128, 512], F32))
        PS1B = E(nc.psum_tensor("PS1B2", [128, 512], BF16))
        sub = ctx.enter_context(contextlib.ExitStack())
        E2 = sub.enter_context
        WQBS = E2(nc.sbuf_tensor("WQBS", [128, 12 * 384], BF16))
        QLS = E2(nc.sbuf_tensor("QLS", [128, 12 * 512], BF16))
        IVC = E2(nc.sbuf_tensor("IVC", [64, 1], F32))
        RSCF = E2(nc.sbuf_tensor("RSCF", [64, 6 * 512], F32))
        PSQP = E2(nc.psum_tensor("PSQP", [128, 512], F32))
        COLS = G["COLS"]

        for qq in range(4):
            m = MASK[:, qq * 512:(qq + 1) * 512]
            b.op("vector", "memset", m, 0.0)
            b.op("gpsimd", "affine_select", m, m, pattern=[[-1, 512]],
                 compare_op=ALU.is_ge, fill=-30000.0, base=qq * 128,
                 channel_multiplier=1)
        b.dma(IVC[0:64, :], ins["invf64"][:, :], eng="gpsimd")
        b.dma(WQBS[:, :], ins["wqbTs"][:, :], eng="gpsimd")
        for p in range(NCORES):
            tc = slice(p * TS, (p + 1) * TS)
            b.dma(KPX1[0:32, tc], agkv_o[p * 640 + 512: p * 640 + 544, :])
            b.dma(KPX1[32:64, tc], agkv_o[p * 640 + 512: p * 640 + 544, :])
            b.dma(KPX2[0:32, tc], agkv_o[p * 640 + 544: p * 640 + 576, :])
            b.dma(KPX2[32:64, tc], agkv_o[p * 640 + 544: p * 640 + 576, :])
        POS = RSCF[0:64, 0:512]
        FR = RSCF[0:64, 512:1024]
        KR = RSCF[0:64, 1024:1536]
        CSF = RSCF[0:64, 1536:2048]
        SNF = RSCF[0:64, 2048:2560]
        TMR = RSCF[0:64, 2560:3072]
        for tb in range(16):
            p, t2 = tb // 2, tb % 2
            gcols = slice(tb * 512, (tb + 1) * 512)
            for kc in range(12):
                b.dma(QLS[:, kc * 512:(kc + 1) * 512],
                      agql_o[p * 1536 + kc * 128: p * 1536 + (kc + 1) * 128,
                             t2 * 512:(t2 + 1) * 512])
            for (oc, (off, sz)) in enumerate(((0, 128), (128, 128),
                                              (256, 64), (320, 64))):
                b.mm_acc(PSQP[0:sz, 0:512],
                         [(WQBS[:, kc * 384 + off: kc * 384 + off + sz],
                           QLS[:, kc * 512:(kc + 1) * 512])
                          for kc in range(12)])
                dst = (QN0, QN1, QX1, QX2)[oc]
                b.ts(dst[0:sz, gcols], PSQP[0:sz, 0:512], SCALE, op0=ALU.mult)
            b.op("gpsimd", "iota", POS, pattern=[[1, 512]], base=tb * 512,
                 channel_multiplier=0, allow_small_or_imprecise_dtypes=True)
            b.ts(FR, POS, IVC[0:64, 0:1], op0=ALU.mult)
            _emit_sincos(b, FR, KR, SNF, CSF, TMR)
            x1, x2 = QX1[0:64, gcols], QX2[0:64, gcols]
            T1, T2 = RSCF[0:64, 0:512], RSCF[0:64, 512:1024]
            T3 = RSCF[0:64, 2560:3072]
            b.tt(T1, x1, CSF, ALU.mult)
            b.tt(T2, x2, SNF, ALU.mult)
            b.tt(T1, T1, T2, ALU.subtract)
            b.tt(T2, x2, CSF, ALU.mult)
            b.tt(T3, x1, SNF, ALU.mult)
            b.tt(x2, T2, T3, ALU.add)
            b.cp(x1, T1)

        sub.close()
        b.barrier()
        KH = E(nc.sbuf_tensor("KH", [128, S], BF16))
        VHT = E(nc.sbuf_tensor("VHT", [128, S], BF16))
        SROW = E(nc.sbuf_tensor("SROW", [128, S], F32))
        QT = E(nc.sbuf_tensor("QT2", [128, 512], BF16))
        OT = E(nc.sbuf_tensor("OT2", [128, 512], BF16))
        KVF2 = E(nc.sbuf_tensor("KVF2", [128, 512], BF16))
        for hh in range(HPC):
            b.dma(QT[:, :], ins["qabsT"][hh * 128:(hh + 1) * 128, :],
                  eng="gpsimd")
            b.dma(OT[:, :], ins["oaT"][hh * 128:(hh + 1) * 128, :],
                  eng="gpsimd")
            QNF = QN0 if hh == 0 else QN1
            for tb in range(16):
                p, t2 = tb // 2, tb % 2
                gcols = slice(tb * 512, (tb + 1) * 512)
                for cc in range(4):
                    b.dma(KVF2[:, 0:512],
                          agkv_o[p * 640 + cc * 128: p * 640 + (cc + 1) * 128,
                                 t2 * 512:(t2 + 1) * 512])
                    b.mm(PSK[:, 0:512], QT[:, cc * 128:(cc + 1) * 128],
                         KVF2[:, 0:512], start=(cc == 0), stop=(cc == 3))
                    b.mm(PSV[:, 0:512], OT[:, cc * 128:(cc + 1) * 128],
                         KVF2[:, 0:512], start=(cc == 0), stop=(cc == 3))
                b.cp(KH[:, gcols], PSK[:, 0:512])
                b.cp(PB[:, 0:512], PSV[:, 0:512])
                for j in range(4):
                    ct = tb * 4 + j
                    b.tp(PS1B[:, 0:128], PB[:, j * 128:(j + 1) * 128],
                         VHT[:, ct * 128:(ct + 1) * 128], G["ID"][:])
            for qi in range(NTF):
                nk = qi // 4 + 1
                qcols = slice(qi * 128, (qi + 1) * 128)
                h32 = slice(hh * 32, (hh + 1) * 32)
                for kb in range(nk):
                    kcols = slice(kb * 512, (kb + 1) * 512)
                    b.mm(PSS[:, 0:512], QNF[:, qcols], KH[:, kcols],
                         start=True, stop=False)
                    b.mm(PSS[:, 0:512], QX1[h32, qcols], KPX1[h32, kcols],
                         start=False, stop=False)
                    b.mm(PSS[:, 0:512], QX2[h32, qcols], KPX2[h32, kcols],
                         start=False, stop=True)
                    if kb == nk - 1:
                        b.tt(PSS[:, 0:512], PSS[:, 0:512],
                             MASK[:, (qi % 4) * 512:(qi % 4 + 1) * 512],
                             ALU.add)
                    b.op("vector", "reduce_max", MX[:, kb:kb + 1],
                         PSS[:, 0:512], axis=AX.X)
                    b.cp(SROW[:, kcols], PSS[:, 0:512])
                b.op("vector", "reduce_max", COLS[:, 0:1], MX[:, 0:nk],
                     axis=AX.X, negate=True)
                for kb in range(nk):
                    kcols = slice(kb * 512, (kb + 1) * 512)
                    b.act(PB[:, kcols], SROW[:, kcols], AF.Exp,
                          bias=COLS[:, 0:1], accum_out=LC[:, kb:kb + 1])
                b.op("vector", "reduce_sum", COLS[:, 1:2], LC[:, 0:nk],
                     axis=AX.X)
                b.op("vector", "reciprocal", COLS[:, 2:3], COLS[:, 1:2])
                nch = nk * 4
                for ct in range(nch):
                    b.tp(PS1B[:, 0:128], PB[:, ct * 128:(ct + 1) * 128],
                         PTB[:, :], G["ID"][:])
                    b.mm(PSO[:, 0:VH], PTB[:, :],
                         VHT[:, ct * 128:(ct + 1) * 128],
                         start=(ct == 0), stop=(ct == nch - 1))
                b.ts(XH[:, :], PSO[:, 0:VH], COLS[:, 2:3], op0=ALU.mult)
                b.cp(XHB[:, :], XH[:, :])
                b.tp(PS1B[:, 0:128], XHB[:, :], PTB[:, :], G["ID"][:])
                b.dma(xhf_d[hh * 128:(hh + 1) * 128, qcols], PTB[:, :])


def emit_partial_proj(nc, b, G, tag, nchunk, src_d, wTs_in, rs_in):
    with contextlib.ExitStack() as ctx:
        E = ctx.enter_context
        XF = E(nc.sbuf_tensor(f"XF{tag}", [128, nchunk * 128], BF16))
        WS = E(nc.sbuf_tensor(f"WS{tag}", [128, nchunk * D], BF16))
        RO = E(nc.sbuf_tensor(f"RO{tag}", [128, D], BF16))
        PSN = [E(nc.psum_tensor(f"PSN{tag}{j}", [128, 512], F32))
               for j in range(4)]
        for ch in range(nchunk):
            b.dma(WS[:, ch * D:(ch + 1) * D],
                  wTs_in[ch * 128:(ch + 1) * 128, :], eng="gpsimd")
        for i in range(NTF):
            tcols = slice(i * 128, (i + 1) * 128)
            for ch in range(nchunk):
                b.dma(XF[:, ch * 128:(ch + 1) * 128],
                      src_d[ch * 128:(ch + 1) * 128, tcols])
            for nb in range(4):
                for ch in range(nchunk):
                    b.mm(PSN[nb][:, 0:512], XF[:, ch * 128:(ch + 1) * 128],
                         WS[:, ch * D + nb * 512: ch * D + (nb + 1) * 512],
                         start=(ch == 0), stop=(ch == nchunk - 1))
            for nb in range(4):
                b.cp(RO[:, nb * 512:(nb + 1) * 512], PSN[nb][:, 0:512])
            b.dma(rs_in[i * 128:(i + 1) * 128, :], RO[:, :])


def emit_resid_ffn(nc, b, G, tag, ins, rs_out, x_src, x_dst, ln_dst):
    with contextlib.ExitStack() as ctx:
        E = ctx.enter_context
        FWT = E(nc.sbuf_tensor(f"FWT{tag}", [128, 16 * D], BF16))
        GB = E(nc.sbuf_tensor(f"GB{tag}", [128, D], F32))
        BB = E(nc.sbuf_tensor(f"BB{tag}", [128, D], F32))
        TMP = E(nc.sbuf_tensor(f"TMP{tag}", [128, D], F32))
        TMP2 = E(nc.sbuf_tensor(f"TMP2{tag}", [128, D], F32))
        XQ = E(nc.sbuf_tensor(f"XQ{tag}", [128, D], F32))
        XQT = E(nc.sbuf_tensor(f"XQT{tag}", [128, D], BF16))
        RB = E(nc.sbuf_tensor(f"RB{tag}", [128, D], BF16))
        GCOL = E(nc.sbuf_tensor(f"GCOL{tag}", [128, 16], F32))
        LNS = E(nc.sbuf_tensor(f"LNS{tag}", [128, 128], BF16))
        XT2 = E(nc.sbuf_tensor(f"XT2{tag}", [128, D], F32))
        PS0 = E(nc.psum_tensor(f"PS0{tag}", [128, 512], F32))
        PS1 = E(nc.psum_tensor(f"PS1{tag}", [128, 512], F32))
        PS1B = E(nc.psum_tensor(f"PS1B{tag}", [128, 512], BF16))
        COLS = G["COLS"]

        for i in range(16):
            b.dma(TMP[:, :], ins["fw"][i * 128:(i + 1) * 128, :])
            b.op("vector", "reduce_sum", GCOL[:, i:i + 1], TMP[:], axis=AX.X,
                 apply_absolute_value=True)
        b.ts(GCOL[:], GCOL[:], 1.0 / D)
        for i in range(16):
            b.dma(TMP[0:1, i * 128:(i + 1) * 128], GCOL[:, i:i + 1],
                  eng="gpsimd")
        b.bcast_rows(GB, TMP[0:1, 0:D])
        b.dma(TMP[0:1, 0:D], ins["fb"][0:1, :], eng="gpsimd")
        b.bcast_rows(BB, TMP[0:1, 0:D])
        b.ts(TMP, GB, 1e-8, op0=ALU.add)
        b.op("vector", "reciprocal", TMP, TMP)
        for i in range(16):
            b.dma(TMP2[:, :], ins["fwT"][i * 128:(i + 1) * 128, :])
            b.tt(TMP2[:], TMP2[:], TMP[:], ALU.mult)
            b.ts(TMP2[:], TMP2[:], RND, RND, op0=ALU.add, op1=ALU.subtract)
            b.ts(TMP2[:], TMP2[:], 1.0, -1.0, op0=ALU.min, op1=ALU.max)
            b.cp(FWT[:, i * D:(i + 1) * D], TMP2[:])

        for i in range(NT):
            xt = XT2[:, :]
            b.dma(xt, x_src[i * 128:(i + 1) * 128, :])
            b.dma(RB[:, :], rs_out[i * 128:(i + 1) * 128, :])
            b.tt(xt, xt, RB[:, :], ALU.add)
            emit_ln_tm(b, xt, TMP2, COLS, D, 1e-5, sub_mean=True,
                       sq_scratch=TMP)
            emit_ln_tm(b, TMP2[:, 0:D], TMP, COLS, D, 1e-8, sub_mean=False)
            b.op("vector", "reduce_max", COLS[:, 4:5], TMP, axis=AX.X,
                 apply_absolute_value=True)
            b.ts(COLS[:, 4:5], COLS[:, 4:5], 1.0 / 127.0)
            b.ts(COLS[:, 5:6], COLS[:, 4:5], 1e-8, op0=ALU.add)
            b.op("vector", "reciprocal", COLS[:, 5:6], COLS[:, 5:6])
            b.ts(XQ, TMP, COLS[:, 5:6], RND, op0=ALU.mult, op1=ALU.add)
            b.ts(XQ, XQ, RND, 127.0, op0=ALU.subtract, op1=ALU.min)
            b.ts(XQ, XQ, -127.0, op0=ALU.max)
            for kc in range(16):
                b.tp(PS1[:, 0:128], XQ[:, kc * 128:(kc + 1) * 128],
                     XQT[:, kc * 128:(kc + 1) * 128], G["IDF"][:])
            for nb in range(D // 512):
                b.mm_acc(PS0[:, 0:512],
                         [(XQT[:, kc * 128:(kc + 1) * 128],
                           FWT[:, kc * D + nb * 512: kc * D + (nb + 1) * 512])
                          for kc in range(16)])
                sl = slice(nb * 512, (nb + 1) * 512)
                b.stt(TMP[:, sl], PS0[:, 0:512], COLS[:, 4:5], GB[:, sl],
                      op0=ALU.mult, op1=ALU.mult)
                b.tt(TMP[:, sl], TMP[:, sl], BB[:, sl], ALU.add)
                b.tt(xt[:, sl], xt[:, sl], TMP[:, sl], ALU.add)
            b.dma(x_dst[i * 128:(i + 1) * 128, :], xt)
            if ln_dst is not None:
                emit_ln_tm(b, xt, TMP2[:, 0:D], COLS, D, 1e-5, sub_mean=True,
                           sq_scratch=TMP)
                b.cp(XQT[:, 0:D], TMP2[:, 0:D])
                for kc in range(16):
                    b.op("tensor", "transpose", PS1B[:, 0:128],
                         XQT[:, kc * 128:(kc + 1) * 128], G["ID"][:])
                    b.cp(LNS[:, :], PS1B[:, 0:128])
                    b.dma(ln_dst[kc * 128:(kc + 1) * 128,
                                 i * 128:(i + 1) * 128], LNS[:, :])


def emit_phase4(nc, b, G, ins, aglnx_o, obf_d):
    SC = 1024
    NSC = S // SC
    ID, IDF, COLS = G["ID"], G["IDF"], G["COLS"]
    with contextlib.ExitStack() as ctx:
        E = ctx.enter_context
        WQ = E(nc.sbuf_tensor("WQ4", [128, 16 * 1544], BF16))
        LNXF = E(nc.sbuf_tensor("LNXF", [128, 16 * 512], BF16))
        CONVF = E(nc.sbuf_tensor("CONVF", [128, 8 * SC], F32))
        ZF = E(nc.sbuf_tensor("ZF", [128, 4 * SC], BF16))
        BAF = E(nc.sbuf_tensor("BAF", [8, SC], F32))
        XEXT = E(nc.sbuf_tensor("XEXT", [128, SC + 3], F32))
        CARRY = E(nc.sbuf_tensor("CARRY", [128, 8 * 3], F32))
        CONVW = E(nc.sbuf_tensor("CONVW", [128, 32], F32))
        QKVTM = E(nc.sbuf_tensor("QKVTM", [128, 8 * 128], F32))
        ZTM = E(nc.sbuf_tensor("ZTM", [128, 4 * 128], BF16))
        BATM = E(nc.sbuf_tensor("BATM", [128, 8], F32))
        QNB = E(nc.sbuf_tensor("QNB", [128, 2 * 128], BF16))
        KNB = E(nc.sbuf_tensor("KNB", [128, 2 * 128], BF16))
        VB = E(nc.sbuf_tensor("VB", [128, 4 * 128], BF16))
        G4 = E(nc.sbuf_tensor("G4", [128, 4], F32))
        BETA4 = E(nc.sbuf_tensor("BETA4", [128, 4], F32))
        NEGB4 = E(nc.sbuf_tensor("NEGB4", [128, 4], F32))
        SF = E(nc.sbuf_tensor("SF", [128, 4 * 128], F32))
        SB = E(nc.sbuf_tensor("SB", [128, 4 * 128], BF16))
        MKK = E(nc.sbuf_tensor("MKK", [128, 128], F32))
        GRAW = E(nc.sbuf_tensor("GRAW", [128, 128], F32))
        ES = E(nc.sbuf_tensor("ES", [128, 128], F32))
        ETI = E(nc.sbuf_tensor("ETI", [128, 128], F32))
        NMAT = E(nc.sbuf_tensor("NMAT", [128, 128], BF16))
        NTMAT = E(nc.sbuf_tensor("NTMAT", [128, 128], BF16))
        NP2 = E(nc.sbuf_tensor("NP2", [128, 128], BF16))
        NTP2 = E(nc.sbuf_tensor("NTP2", [128, 128], BF16))
        UB = E(nc.sbuf_tensor("UB", [128, 128], BF16))
        VBETA = E(nc.sbuf_tensor("VBETA", [128, 128], F32))
        KF = E(nc.sbuf_tensor("KF", [128, 128], BF16))
        QF = E(nc.sbuf_tensor("QF", [128, 128], BF16))
        KSC_ = E(nc.sbuf_tensor("KSC", [128, 128], BF16))
        KSF = E(nc.sbuf_tensor("KSF", [128, 128], BF16))
        KRB = E(nc.sbuf_tensor("KRB", [128, 128], BF16))
        GTB = E(nc.sbuf_tensor("GTB", [128, 128], BF16))
        OFF = E(nc.sbuf_tensor("OFF", [128, 128], F32))
        OG = E(nc.sbuf_tensor("OG", [128, 128], BF16))
        GROW = E(nc.sbuf_tensor("GROW", [1, 128], F32))
        SROW1 = E(nc.sbuf_tensor("SROW1", [1, 128], F32))
        SCOL = E(nc.sbuf_tensor("SCOL", [128, 1], F32))
        NSCOL = E(nc.sbuf_tensor("NSCOL", [128, 1], F32))
        BSC = E(nc.sbuf_tensor("BSC", [128, 1], F32))
        BCOL = E(nc.sbuf_tensor("BCOL", [128, 1], F32))
        RCOL = E(nc.sbuf_tensor("RCOL", [128, 1], F32))
        ONES1 = E(nc.sbuf_tensor("ONES1", [1, 128], F32))
        C1R = E(nc.sbuf_tensor("C1R", [1, 1], F32))
        E127 = E(nc.sbuf_tensor("E127", [128, 128], F32))
        MADDA = E(nc.sbuf_tensor("MADDA", [128, 128], F32))
        MADDB = E(nc.sbuf_tensor("MADDB", [128, 128], F32))
        NEGA = E(nc.sbuf_tensor("NEGA", [128, 4], F32))
        DTB = E(nc.sbuf_tensor("DTB", [128, 4], F32))
        NWB = E(nc.sbuf_tensor("NWB", [128, 128], F32))
        PSQ = E(nc.psum_tensor("PSQ", [128, 512], F32))
        PSOU = E(nc.psum_tensor("PSOU", [128, 512], F32))
        PSA = E(nc.psum_tensor("PSA4", [128, 512], F32))
        PSM = E(nc.psum_tensor("PSM", [128, 512], F32))
        PSM2 = E(nc.psum_tensor("PSM2", [128, 512], F32))
        PS1 = E(nc.psum_tensor("PS14", [128, 512], F32))
        PS1B = E(nc.psum_tensor("PS1B4", [128, 512], BF16))

        b.op("vector", "memset", ONES1[0:1, :], 1.0)
        b.op("vector", "memset", C1R[0:1, :], 1.0)
        b.op("vector", "memset", E127[:], 1.0)
        b.op("gpsimd", "affine_select", E127[:], E127[:], pattern=[[0, 128]],
             compare_op=ALU.is_ge, fill=0.0, base=-127, channel_multiplier=1)
        b.op("vector", "memset", MADDA[:], 0.0)
        b.op("gpsimd", "affine_select", MADDA[:], MADDA[:],
             pattern=[[-1, 128]], compare_op=ALU.is_ge, fill=1e4, base=-1,
             channel_multiplier=1)
        b.op("vector", "memset", MADDB[:], 0.0)
        b.op("gpsimd", "affine_select", MADDB[:], MADDB[:],
             pattern=[[1, 128]], compare_op=ALU.is_ge, fill=-1e4, base=0,
             channel_multiplier=-1)
        b.op("vector", "memset", CARRY[:], 0.0)
        b.op("vector", "memset", SF[:], 0.0)
        b.op("vector", "memset", SB[:], 0.0)
        for kc in range(16):
            b.dma(WQ[:, kc * 1544:(kc + 1) * 1544],
                  ins["wq"][kc * 128:(kc + 1) * 128, :], eng="gpsimd")
        b.dma(CONVW[:, :], ins["convw"][:, :])
        b.dma(NEGA[0:1, :], ins["alog"][0:1, :])
        b.act(NEGA[0:1, :], NEGA[0:1, :], AF.Exp)
        b.ts(NEGA[0:1, :], NEGA[0:1, :], -1.0)
        b.bcast_rows(NEGA, NEGA[0:1, :])
        b.dma(DTB[0:1, :], ins["dtb"][0:1, :])
        b.bcast_rows(DTB, DTB[0:1, :])
        b.dma(NWB[0:1, :], ins["normw"][0:1, :])
        b.bcast_rows(NWB, NWB[0:1, :])

        for sc in range(NSC):
            for t2 in range(2):
                for kc in range(16):
                    b.dma(LNXF[:, kc * 512:(kc + 1) * 512],
                          aglnx_o[sc * D + kc * 128: sc * D + (kc + 1) * 128,
                                  t2 * 512:(t2 + 1) * 512])
                for cc in range(12):
                    b.mm_acc(PSQ[:, 0:512],
                             [(WQ[:, kc * 1544 + cc * 128:
                                  kc * 1544 + (cc + 1) * 128],
                               LNXF[:, kc * 512:(kc + 1) * 512])
                              for kc in range(16)])
                    if cc < 8:
                        b.cp(CONVF[:, cc * SC + t2 * 512:
                                   cc * SC + (t2 + 1) * 512], PSQ[:, 0:512])
                    else:
                        zc = cc - 8
                        b.act(ZF[:, zc * SC + t2 * 512:
                                 zc * SC + (t2 + 1) * 512], PSQ[:, 0:512],
                              AF.Silu)
                b.mm_acc(PSQ[0:8, 0:512],
                         [(WQ[:, kc * 1544 + 1536: kc * 1544 + 1544],
                           LNXF[:, kc * 512:(kc + 1) * 512])
                          for kc in range(16)])
                b.cp(BAF[0:8, t2 * 512:(t2 + 1) * 512], PSQ[0:8, 0:512])
            for chm in range(8):
                b.cp(XEXT[:, 0:3], CARRY[:, chm * 3:(chm + 1) * 3])
                b.cp(XEXT[:, 3:SC + 3], CONVF[:, chm * SC:(chm + 1) * SC])
                b.cp(CARRY[:, chm * 3:(chm + 1) * 3], XEXT[:, SC:SC + 3])
                dst = CONVF[:, chm * SC:(chm + 1) * SC]
                b.ts(dst, XEXT[:, 0:SC], CONVW[:, chm * 4:chm * 4 + 1],
                     op0=ALU.mult)
                for j in range(1, 4):
                    b.stt(dst, XEXT[:, j:j + SC],
                          CONVW[:, chm * 4 + j:chm * 4 + j + 1], dst,
                          op0=ALU.mult, op1=ALU.add)
                b.act(dst, dst, AF.Silu)

            for ti in range(8):
                ck = sc * 8 + ti
                rows = slice(ck * 128, (ck + 1) * 128)
                for chm in range(8):
                    b.tp(PS1[:, 0:128],
                         CONVF[:, chm * SC + ti * 128:
                               chm * SC + (ti + 1) * 128],
                         QKVTM[:, chm * 128:(chm + 1) * 128], IDF[:])
                for vh in range(4):
                    b.tp(PS1B[:, 0:128],
                         ZF[:, vh * SC + ti * 128: vh * SC + (ti + 1) * 128],
                         ZTM[:, vh * 128:(vh + 1) * 128], ID[:])
                b.op("tensor", "transpose", PS1[:, 0:8],
                     BAF[0:8, ti * 128:(ti + 1) * 128], IDF[0:8, 0:8])
                b.cp(BATM[:, 0:8], PS1[:, 0:8])
                for j in range(4):
                    sl = QKVTM[:, j * 128:(j + 1) * 128]
                    b.act(OFF[:, :], sl, AF.Square, accum_out=COLS[:, 0:1])
                    b.ts(COLS[:, 0:1], COLS[:, 0:1], 1e-6, op0=ALU.add)
                    b.act(COLS[:, 1:2], COLS[:, 0:1], AF.Sqrt)
                    b.op("vector", "reciprocal", COLS[:, 2:3], COLS[:, 1:2])
                    if j < 2:
                        b.ts(QNB[:, j * 128:(j + 1) * 128], sl, COLS[:, 2:3],
                             float(DK ** -0.5), op0=ALU.mult, op1=ALU.mult)
                    else:
                        b.ts(KNB[:, (j - 2) * 128:(j - 1) * 128], sl,
                             COLS[:, 2:3], op0=ALU.mult)
                for vh in range(4):
                    b.cp(VB[:, vh * 128:(vh + 1) * 128],
                         QKVTM[:, (4 + vh) * 128:(5 + vh) * 128])
                for vh in range(4):
                    kh = vh // 2
                    bcol = BATM[:, kh * 4 + vh % 2: kh * 4 + vh % 2 + 1]
                    acol = BATM[:, kh * 4 + 2 + vh % 2: kh * 4 + 3 + vh % 2]
                    t_ = COLS[:, 3:4]
                    b.tt(t_, acol, DTB[:, vh:vh + 1], ALU.add)
                    b.act(COLS[:, 4:5], t_, AF.Abs)
                    b.act(COLS[:, 4:5], COLS[:, 4:5], AF.Exp, scale=-1.0)
                    b.act(COLS[:, 4:5], COLS[:, 4:5], AF.Ln, bias=1.0)
                    b.ts(COLS[:, 5:6], t_, 0.0, op0=ALU.max)
                    b.tt(COLS[:, 4:5], COLS[:, 4:5], COLS[:, 5:6], ALU.add)
                    b.tt(G4[:, vh:vh + 1], COLS[:, 4:5], NEGA[:, vh:vh + 1],
                         ALU.mult)
                    b.act(BETA4[:, vh:vh + 1], bcol, AF.Sigmoid)
                b.ts(NEGB4[:], BETA4[:], -1.0)

                for vh in range(4):
                    kh = vh // 2
                    ktm = KNB[:, kh * 128:(kh + 1) * 128]
                    qtm = QNB[:, kh * 128:(kh + 1) * 128]
                    if vh % 2 == 0:
                        b.tp(PS1B[:, 0:128], ktm, KF[:, :], ID[:])
                        b.tp(PS1B[:, 0:128], qtm, QF[:, :], ID[:])
                        b.mm(PSM[:, 0:128], KF[:, :], KF[:, :], True, True)
                        b.cp(MKK[:, :], PSM[:, 0:128])
                        b.mm(PSM[:, 0:128], KF[:, :], QF[:, :], True, True)
                        b.cp(GRAW[:, :], PSM[:, 0:128])
                    b.op("tensor", "transpose", PS1[0:1, 0:128],
                         G4[:, vh:vh + 1], IDF[:])
                    b.cp(GROW[0:1, :], PS1[0:1, 0:128])
                    b.op("vector", "tensor_tensor_scan", SROW1[0:1, :],
                         GROW[0:1, :], GROW[0:1, :], 0.0, op0=ALU.add,
                         op1=ALU.bypass)
                    b.mm(PSOU[:, 0:1], SROW1[0:1, :], C1R[0:1, 0:1], True,
                         True)
                    b.cp(SCOL[:, :], PSOU[:, 0:1])
                    b.ts(NSCOL[:], SCOL[:], -1.0)
                    b.mm(PSOU[:, 0:1], E127[:, :], SCOL[:, :], True, True)
                    b.cp(BSC[:, :], PSOU[:, 0:1])
                    b.act(BCOL[:, :], BSC[:, :], AF.Exp)
                    b.act(RCOL[:, :], SCOL[:, :], AF.Exp, scale=-1.0,
                          bias=BSC[:, 0:1])
                    b.mm(PSOU[:, 0:128], ONES1[0:1, :], SROW1[0:1, :], True,
                         False)
                    b.mm(PSOU[:, 0:128], IDF[:, :], MADDA[:, :], False, True)
                    b.act(ES[:, :], PSOU[:, 0:128], AF.Exp, scale=-1.0,
                          bias=SCOL[:, 0:1])
                    b.mm(PSOU[:, 0:128], ONES1[0:1, :], SROW1[0:1, :], True,
                         False)
                    b.mm(PSOU[:, 0:128], IDF[:, :], MADDB[:, :], False, True)
                    b.act(ETI[:, :], PSOU[:, 0:128], AF.Exp, scale=1.0,
                          bias=NSCOL[:, 0:1])
                    b.stt(NMAT[:, :], MKK[:, :], NEGB4[:, vh:vh + 1], ES[:, :],
                          op0=ALU.mult, op1=ALU.mult)
                    b.tp(PS1B[:, 0:128], NMAT[:, :], NTMAT[:, :], ID[:])
                    b.ts(KSC_[:, :], ktm, BCOL[:, 0:1], op0=ALU.mult)
                    b.tp(PS1B[:, 0:128], KSC_[:, :], KSF[:, :], ID[:])
                    b.mm(PSM[:, 0:128], KSF[:, :],
                         SB[:, vh * 128:(vh + 1) * 128], True, True)
                    b.ts(VBETA[:, :], VB[:, vh * 128:(vh + 1) * 128],
                         BETA4[:, vh:vh + 1], op0=ALU.mult)
                    b.stt(UB[:, :], PSM[:, 0:128], NEGB4[:, vh:vh + 1],
                          VBETA[:, :], op0=ALU.mult, op1=ALU.add)
                    npow, ntpow = NMAT, NTMAT
                    np2, ntp2 = NP2, NTP2
                    for lvl in range(7):
                        b.mm(PSA[:, 0:128], ntpow[:, :], UB[:, :], True, False)
                        b.mm(PSA[:, 0:128], ID[:, :], UB[:, :], False, True)
                        b.cp(UB[:, :], PSA[:, 0:128])
                        if lvl < 6:
                            b.mm(PSA[:, 0:128], npow[:, :], ntpow[:, :], True,
                                 True)
                            b.cp(ntp2[:, :], PSA[:, 0:128])
                            if lvl < 5:
                                b.mm(PSA[:, 0:128], ntpow[:, :], npow[:, :],
                                     True, True)
                                b.cp(np2[:, :], PSA[:, 0:128])
                            npow, np2 = np2, npow
                            ntpow, ntp2 = ntp2, ntpow
                    b.tt(GTB[:, :], GRAW[:, :], ETI[:, :], ALU.mult)
                    b.mm(PSM[:, 0:128], QF[:, :],
                         SB[:, vh * 128:(vh + 1) * 128], True, True)
                    b.ts(OFF[:, :], PSM[:, 0:128], BCOL[:, 0:1], op0=ALU.mult)
                    b.mm(PSM2[:, 0:128], GTB[:, :], UB[:, :], True, True)
                    b.tt(OFF[:, :], OFF[:, :], PSM2[:, 0:128], ALU.add)
                    b.ts(KRB[:, :], ktm, RCOL[:, 0:1], op0=ALU.mult)
                    b.mm(PSM[:, 0:128], KRB[:, :], UB[:, :], True, True)
                    b.stt(SF[:, vh * 128:(vh + 1) * 128],
                          SF[:, vh * 128:(vh + 1) * 128], BCOL[:, 0:1],
                          PSM[:, 0:128], op0=ALU.mult, op1=ALU.add)
                    b.cp(SB[:, vh * 128:(vh + 1) * 128],
                         SF[:, vh * 128:(vh + 1) * 128])
                    b.act(VBETA[:, :], OFF[:, :], AF.Square,
                          accum_out=COLS[:, 0:1])
                    b.ts(COLS[:, 0:1], COLS[:, 0:1], 1.0 / DV)
                    b.ts(COLS[:, 0:1], COLS[:, 0:1], 1e-6, op0=ALU.add)
                    b.act(COLS[:, 1:2], COLS[:, 0:1], AF.Sqrt)
                    b.op("vector", "reciprocal", COLS[:, 2:3], COLS[:, 1:2])
                    b.ts(OFF[:, :], OFF[:, :], COLS[:, 2:3], op0=ALU.mult)
                    b.tt(OFF[:, :], OFF[:, :], NWB[:, :], ALU.mult)
                    b.tt(OG[:, :], OFF[:, :], ZTM[:, vh * 128:(vh + 1) * 128],
                         ALU.mult)
                    b.tp(PS1B[:, 0:128], OG[:, :], KSF[:, :], ID[:])
                    b.dma(obf_d[vh * 128:(vh + 1) * 128, rows], KSF[:, :])


def build_fused():
    nc = bass.Bass("TRN2", target_bir_lowering=False, debug=False)
    io = lambda n, s, dt, k="ExternalInput": nc.dram_tensor(
        n, s, dt, kind=k).ap()
    ins = {
        "x_in": io("x_in", [TS, D], F32),
        "pos": io("pos", [TS, 1], F32),
        "invf": io("invf", [1, 32], F32),
        "invf64": io("invf64", [64, 1], F32),
        "wqaT": io("wqaT", [D, Q_LORA], F32),
        "qlnw": io("qlnw", [1, Q_LORA], F32),
        "wqbTs": io("wqbTs", [128, 12 * 384], F32),
        "wkvaT": io("wkvaT", [D, KV_LORA + ROPE_D], F32),
        "kvlnw": io("kvlnw", [1, KV_LORA], F32),
        "qabsT": io("qabsT", [HPC * 128, 4 * NOPE], F32),
        "oaT": io("oaT", [HPC * 128, 4 * VH], F32),
        "woTs": io("woTs", [HPC * VH, D], F32),
        "fw": io("fw", [D, D], F32),
        "fwT": io("fwT", [D, D], F32),
        "fb": io("fb", [1, D], F32),
        "wouTs": io("wouTs", [VPC * DV, D], F32),
        "wq": io("wq", [D, 1544], F32),
        "convw": io("convw", [128, 32], F32),
        "alog": io("alog", [1, 4], F32),
        "dtb": io("dtb", [1, 4], F32),
        "normw": io("normw", [1, DV], F32),
    }
    x_out = io("x_out", [TS, D], F32, "ExternalOutput")

    dram = lambda n, s, dt, shared=False: nc.dram_tensor(
        n, s, dt, addr_space=("Shared" if shared else "Local"))
    fence_i = dram("fence_i", [16, 16], F32)
    fence_o = dram("fence_o", [NCORES * 16, 16], F32, True)
    agql_i = dram("agql_i", [Q_LORA, TS], BF16)
    agql_o = dram("agql_o", [NCORES * Q_LORA, TS], BF16, True)
    agkv_i = dram("agkv_i", [640, TS], BF16)
    agkv_o = dram("agkv_o", [NCORES * 640, TS], BF16, True)
    aglnx_i = dram("aglnx_i", [D, TS], BF16)
    aglnx_o = dram("aglnx_o", [NCORES * D, TS], BF16, True)
    xhf_d = dram("xhf_d", [HPC * VH, S], BF16)
    obf_d = dram("obf_d", [VPC * DV, S], BF16)
    rs1_i = dram("rs1_i", [S, D], BF16)
    rs1_o = dram("rs1_o", [TS, D], BF16)
    rs2_i = dram("rs2_i", [S, D], BF16)
    rs2_o = dram("rs2_o", [TS, D], BF16)
    x2_d = dram("x2_d", [TS, D], F32)

    with contextlib.ExitStack() as gstack:
        GE = gstack.enter_context
        ID = GE(nc.sbuf_tensor("ID", [128, 128], BF16))
        IDF = GE(nc.sbuf_tensor("IDF", [128, 128], F32))
        COLS = GE(nc.sbuf_tensor("COLS", [128, 8], F32))
        C1 = GE(nc.sbuf_tensor("C1", [128, 1], F32))
        sems = {d: GE(nc.semaphore(d)) for d in
                ("c_tensor", "c_vector", "c_scalar", "c_gpsimd", "c_sync",
                 "d_sync", "d_gpsimd", "d_scalar")}
        block = GE(nc.Block())
        G = {"ID": ID, "IDF": IDF, "COLS": COLS}
        b = B(nc)
        emit_identity(b, ID)
        emit_identity(b, IDF)
        b.op("vector", "memset", C1[:], 1.0)
        nc.const_aps.register(F32, 1.0, C1[:, 0:1].opt())

        _s = len(b.seq)
        def fence():
            b.coll("AllGather", ALU.bypass, fence_i, fence_o)

        emit_phase1(nc, b, G, ins, agql_i, agkv_i)
        b.mark_phase("p1", _s, len(b.seq))
        b.coll("AllGather", ALU.bypass, agql_i, agql_o)
        b.coll("AllGather", ALU.bypass, agkv_i, agkv_o)
        fence()
        b.barrier()
        _s = len(b.seq)
        emit_phase2(nc, b, G, ins, agql_o, agkv_o, xhf_d)
        b.mark_phase("p2", _s, len(b.seq))
        b.barrier()
        _s = len(b.seq)
        emit_partial_proj(nc, b, G, "2b", HPC, xhf_d, ins["woTs"], rs1_i)
        b.mark_phase("p2b", _s, len(b.seq))
        b.coll("ReduceScatter", ALU.add, rs1_i, rs1_o)
        fence()
        b.barrier()
        _s = len(b.seq)
        emit_resid_ffn(nc, b, G, "3", ins, rs1_o, ins["x_in"], x2_d, aglnx_i)
        b.mark_phase("p3", _s, len(b.seq))
        b.coll("AllGather", ALU.bypass, aglnx_i, aglnx_o)
        fence()
        b.barrier()
        _s = len(b.seq)
        emit_phase4(nc, b, G, ins, aglnx_o, obf_d)
        b.mark_phase("p4", _s, len(b.seq))
        b.barrier()
        _s = len(b.seq)
        emit_partial_proj(nc, b, G, "4b", VPC, obf_d, ins["wouTs"], rs2_i)
        b.mark_phase("p4b", _s, len(b.seq))
        b.coll("ReduceScatter", ALU.add, rs2_i, rs2_o)
        fence()
        b.barrier()
        _s = len(b.seq)
        emit_resid_ffn(nc, b, G, "5", ins, rs2_o, x2_d, x_out, None)
        b.mark_phase("p5", _s, len(b.seq))

        b.emit(block, sems)
    return nc


_ROPE_PERM = np.concatenate([np.arange(0, 64, 2), np.arange(1, 64, 2)])
_NC_CACHE = {}


def _stage_lhsT(A):
    return np.ascontiguousarray(
        A.T.reshape(4, 128, 128).transpose(1, 0, 2).reshape(128, 512))


def kernel(L0_attn_a_hidden_states, L0_attn_a_wq_a, L0_attn_a_q_a_ln_w,
           L0_attn_a_wq_b, L0_attn_a_wkv_a, L0_attn_a_kv_a_ln_w,
           L0_attn_a_wkv_b, L0_attn_a_wo, L0_attn_b_W_qkvz, L0_attn_b_W_ba,
           L0_attn_b_conv_weight, L0_attn_b_A_log, L0_attn_b_dt_bias,
           L0_attn_b_norm_weight, L0_attn_b_W_out, L0_ffn_weight,
           L0_ffn_bias):
    cores = list(range(NCORES))
    asc = np.ascontiguousarray
    x = asc(np.asarray(L0_attn_a_hidden_states)[0])

    perm = _ROPE_PERM
    wqb = np.asarray(L0_attn_a_wq_b)
    wkva = np.asarray(L0_attn_a_wkv_a)
    rows = list(range(KV_LORA)) + [KV_LORA + p for p in perm]
    wkvaT = asc(wkva[rows, :].T)
    wqaT = asc(np.asarray(L0_attn_a_wq_a).T)
    invf = (1.0 / 10000.0 ** (np.arange(0, ROPE_D, 2, dtype=np.float32)
                              / ROPE_D)).astype(np.float32)
    invf64 = asc(np.concatenate([invf, invf]).reshape(64, 1))
    qlnw = asc(np.asarray(L0_attn_a_q_a_ln_w).reshape(1, -1))
    kvlnw = asc(np.asarray(L0_attn_a_kv_a_ln_w).reshape(1, -1))
    w_kvb = np.asarray(L0_attn_a_wkv_b).reshape(NH, NOPE + VH, KV_LORA)
    woT = np.asarray(L0_attn_a_wo).T
    fw = asc(np.asarray(L0_ffn_weight))
    fwT = asc(fw.T)
    fb = asc(np.asarray(L0_ffn_bias).reshape(1, -1))
    wout = np.asarray(L0_attn_b_W_out)
    Wq = np.asarray(L0_attn_b_W_qkvz)
    Wba = np.asarray(L0_attn_b_W_ba)
    cw = np.asarray(L0_attn_b_conv_weight)
    alog = np.asarray(L0_attn_b_A_log)
    dtb = np.asarray(L0_attn_b_dt_bias)
    normw = asc(np.asarray(L0_attn_b_norm_weight).reshape(1, DV))

    if "fused" not in _NC_CACHE:
        _NC_CACHE["fused"] = build_fused()
    nc = _NC_CACHE["fused"]

    maps = []
    for c in cores:
        sl = slice(c * TS, (c + 1) * TS)
        h0, h1 = HPC * c, HPC * c + 1
        cols = (list(range(h0 * QH, h0 * QH + NOPE))
                + list(range(h1 * QH, h1 * QH + NOPE))
                + [h0 * QH + NOPE + p for p in perm[:32]]
                + [h1 * QH + NOPE + p for p in perm[:32]]
                + [h0 * QH + NOPE + p for p in perm[32:]]
                + [h1 * QH + NOPE + p for p in perm[32:]])
        wqbTs = asc(wqb[cols, :].T.reshape(12, 128, 384)
                    .transpose(1, 0, 2).reshape(128, 12 * 384))
        qabsT = np.concatenate(
            [_stage_lhsT(w_kvb[HPC * c + hh, :NOPE, :]) for hh in range(HPC)],
            0).astype(np.float32)
        oaT = np.concatenate(
            [_stage_lhsT(w_kvb[HPC * c + hh, VH:, :]) for hh in range(HPC)],
            0).astype(np.float32)
        woTs = asc(woT[c * HPC * VH:(c + 1) * HPC * VH, :])
        wouTs = asc(wout[c * VPC * DV:(c + 1) * VPC * DV, :])
        khs = [KPC * c, KPC * c + 1]
        qc = np.concatenate([np.arange(kh * 768, kh * 768 + 128)
                             for kh in khs])
        kc_ = np.concatenate([np.arange(kh * 768 + 128, kh * 768 + 256)
                              for kh in khs])
        vc = np.concatenate([np.arange(kh * 768 + 256, kh * 768 + 512)
                             for kh in khs])
        zc = np.concatenate([np.arange(kh * 768 + 512, kh * 768 + 768)
                             for kh in khs])
        bac = np.concatenate([np.arange(kh * 4, kh * 4 + 4) for kh in khs])
        wq_stage = asc(np.concatenate(
            [Wq[:, qc], Wq[:, kc_], Wq[:, vc], Wq[:, zc], Wba[:, bac]], 1))
        conv_rows = np.concatenate([
            np.arange(KPC * c * 128, (KPC * c + 2) * 128),
            2048 + np.arange(KPC * c * 128, (KPC * c + 2) * 128),
            4096 + np.arange(VPC * c * 128, (VPC * c + 4) * 128)])
        convw = asc(cw[conv_rows].reshape(8, 128, 4)
                    .transpose(1, 0, 2).reshape(128, 32))
        maps.append({
            "x_in": asc(x[sl]),
            "pos": np.arange(c * TS, (c + 1) * TS,
                             dtype=np.float32).reshape(-1, 1),
            "invf": asc(invf.reshape(1, 32)), "invf64": invf64,
            "wqaT": wqaT, "qlnw": qlnw, "wqbTs": wqbTs,
            "wkvaT": wkvaT, "kvlnw": kvlnw, "qabsT": qabsT, "oaT": oaT,
            "woTs": woTs, "fw": fw, "fwT": fwT, "fb": fb, "wouTs": wouTs,
            "wq": wq_stage, "convw": convw,
            "alog": asc(alog[VPC * c:VPC * c + 4].reshape(1, 4)),
            "dtb": asc(dtb[VPC * c:VPC * c + 4].reshape(1, 4)),
            "normw": normw})
    res = run_bass_kernel_spmd(nc, maps, core_ids=cores)
    x4 = np.concatenate([r["x_out"] for r in res.results], 0)
    return x4.reshape(1, S, D)


# version 04 (final)
# speedup vs baseline: 1.2522x; 1.2522x over previous
import contextlib
import numpy as np
import ml_dtypes
import concourse.bass as bass
import concourse.mybir as mybir
from concourse.bass_utils import run_bass_kernel_spmd
from concourse.bass_types import AP as _AP


def _ap(a):
    if a is not None and not isinstance(a, _AP) and hasattr(a, "ap"):
        return a.ap()
    return a

F32 = mybir.dt.float32
BF16 = mybir.dt.bfloat16
AF = mybir.ActivationFunctionType
ALU = mybir.AluOpType
AX = mybir.AxisListType
RND = 12582912.0
BF = ml_dtypes.bfloat16

S, D = 8192, 2048
NCORES = 8
TS = S // NCORES
NT = TS // 128
NTF = S // 128

NH = 16
HPC = NH // NCORES
Q_LORA, KV_LORA = 1536, 512
NOPE, ROPE_D, VH = 128, 64, 128
QH = NOPE + ROPE_D
SCALE = QH ** -0.5

BK, BV, DK, DV, KCONV = 16, 32, 128, 128, 4
KPC = BK // NCORES
VPC = BV // NCORES

PI_HI = float(np.float32(6.2831855))
PI_LO = float(np.float64(2 * np.pi) - np.float64(np.float32(6.2831855)))
INV2PI = float(np.float32(1.0 / (2 * np.pi)))
PI_CLAMP = 3.14159255
RG = [list(range(NCORES))]


SERIAL_PHASES = set()
STRICT_GPSIMD = False
STRICT_DMA = False


class B:

    ENGS = ("sync", "tensor", "vector", "scalar", "gpsimd")

    def __init__(self, nc):
        self.nc = nc
        self.seq = []
        self.k = 0
        self.phase_marks = {}

    def mark_phase(self, name, start, end):
        self.phase_marks[name] = (start, end)

    def op(self, eng, method, *args, inc=1, **kwargs):
        args = tuple(_ap(a) for a in args)
        kwargs = {k_: (_ap(v) if k_ in ("out", "in_", "bias", "scale",
                                        "accum_out") else v)
                  for k_, v in kwargs.items()}
        self.seq.append((eng, self.k, method, args, kwargs, inc))
        self.k += inc

    def dma(self, out, in_, eng="sync"):
        self.op(eng, "dma_start", inc=16, out=out, in_=in_)

    def ts(self, out, in0, s1, s2=None, op0=ALU.mult, op1=ALU.bypass,
           eng="vector"):
        if s2 is None:
            self.op(eng, "tensor_scalar", out, in0, s1, None, op0=op0)
        else:
            self.op(eng, "tensor_scalar", out, in0, s1, s2, op0=op0, op1=op1)

    def tt(self, out, in0, in1, alu, eng="vector"):
        self.op(eng, "tensor_tensor", out, in0, in1, op=alu)

    def stt(self, out, in0, scalar, in1, op0, op1, eng="vector"):
        self.op(eng, "scalar_tensor_tensor", out, in0, scalar, in1,
                op0=op0, op1=op1)

    def cp(self, out, in_, eng="vector"):
        self.op(eng, "copy" if eng == "scalar" else "tensor_copy", out, in_)

    def act(self, out, in_, func, bias=0.0, scale=1.0, accum_out=None,
            eng="scalar"):
        kw = dict(bias=bias, scale=scale)
        if accum_out is not None:
            kw["accum_out"] = accum_out
        self.op(eng, "activation", out, in_, func, **kw)

    def mm(self, psum, lhsT, rhs, start, stop, eng="tensor"):
        self.op(eng, "matmul", psum, lhsT, rhs, start=start, stop=stop)

    def mm_acc(self, psum, ops):
        n = len(ops)
        for i, (l, r) in enumerate(ops):
            self.mm(psum, l, r, i == 0, i == n - 1)

    def tp(self, psum128, src, dst, ident, evac_eng="vector"):
        self.op("tensor", "transpose", psum128, src, ident)
        self.cp(dst, psum128, eng=evac_eng)

    def bcast_rows(self, dst, src_row, nrows=128, eng="gpsimd"):
        self.dma(dst[0:1, :], src_row[0:1, :], eng=eng)
        n = 1
        while n < nrows:
            m = min(n, nrows - n)
            self.dma(dst[n:n + m, :], dst[0:m, :], eng=eng)
            n += m

    def barrier(self):
        self.seq.append(("sync", self.k, "__barrier__", (), {}, 0))

    def coll(self, kind, alu, in_t, out_t):
        self.op("gpsimd", "collective_compute", kind, alu,
                replica_groups=RG, ins=[in_t.ap().opt()],
                outs=[out_t.ap().opt()])

    _WRITE_POS = {
        "matmul": ((0,), True), "transpose": ((0,), False),
        "tensor_scalar": ((0,), False), "tensor_tensor": ((0,), False),
        "scalar_tensor_tensor": ((0,), False), "tensor_copy": ((0,), False),
        "tensor_tensor_scan": ((0,), False), "memset": ((0,), False),
        "reduce_max": ((0,), False), "reduce_sum": ((0,), False),
        "reciprocal": ((0,), False), "iota": ((0,), False),
        "affine_select": ((0,), False), "activation": ((0,), False),
        "tensor_single_scalar": ((0,), False), "copy": ((0,), False),
    }

    @staticmethod
    def _span(ap):
        if not isinstance(ap, _AP):
            return None
        off = ap.offset
        if not isinstance(off, int):
            raise RuntimeError("dynamic AP offsets unsupported in tracker")
        ext = 1
        for s_, c_ in ap.ap:
            ext += (c_ - 1) * abs(s_)
        return (ap.tensor.name, off, off + ext)

    def _rw(self, meth, args, kw):
        wr, rd = [], []
        if meth == "dma_start":
            wr.append(self._span(kw["out"]))
            rd.append(self._span(kw["in_"]))
        elif meth == "dma_start_transpose":
            wr.append(self._span(args[0]))
            rd.append(self._span(args[1]))
        elif meth == "collective_compute":
            for a in kw.get("ins", []):
                rd.append(self._span(a))
                wr.append(self._span(a))
            for a in kw.get("outs", []):
                wr.append(self._span(a))
        else:
            wpos, rmw = self._WRITE_POS.get(meth, ((), False))
            for i, a in enumerate(args):
                sp = self._span(a)
                if sp is None:
                    continue
                if i in wpos:
                    wr.append(sp)
                    if rmw:
                        rd.append(sp)
                else:
                    rd.append(sp)
            for k_, v in kw.items():
                sp = self._span(v)
                if sp is None:
                    continue
                if k_ == "accum_out":
                    wr.append(sp)
                else:
                    rd.append(sp)
        return [s for s in wr if s], [s for s in rd if s]

    def check_hb(self):
        n = len(self.seq)
        is_dma = [m in ("dma_start", "dma_start_transpose")
                  for (_, _, m, _, _, _) in self.seq]
        is_marker = [m == "__barrier__" for (_, _, m, _, _, _) in self.seq]
        dom = [("d_" + e) if is_dma[i] else ("c_" + e)
               for i, (e, _, m, _, _, _) in enumerate(self.seq)]
        deps, marked, cnt, _run = self._deps_pass()
        seqno = [0] * n
        runall = {}
        for i in range(n):
            if is_marker[i]:
                continue
            runall[dom[i]] = runall.get(dom[i], 0) + 1
            seqno[i] = runall[dom[i]]
        cnt2seq = {}
        for i in range(n):
            if marked[i]:
                cnt2seq.setdefault(dom[i], {})[cnt[i]] = seqno[i]
        clock = [None] * n
        lastate = {}
        for i in range(n):
            if is_marker[i]:
                continue
            eng = self.seq[i][0]
            c = dict(lastate.get(eng + "_clk", {}))
            for j in deps[i]:
                d_ = dom[j]
                s_ = cnt2seq.get(d_, {}).get(cnt[j], 0)
                if s_ > c.get(d_, 0):
                    c[d_] = s_
                cj = clock[j]
                if cj:
                    for d2, s2 in cj.items():
                        if s2 > c.get(d2, 0):
                            c[d2] = s2
            clock[i] = c
            lastate[eng + "_clk"] = c
        acc = {}
        bad = []
        for i in range(n):
            if is_marker[i]:
                continue
            (_, _, meth, args, kw, _) = self.seq[i]
            wr, rd = self._rw(meth, args, kw)
            ci = clock[i]
            eng = self.seq[i][0]
            def ordered(j):
                d_ = dom[j]
                return ci.get(d_, 0) >= seqno[j]
            for (t, lo, hi) in rd + wr:
                for (alo, ahi, aw, aj) in acc.get(t, ()):
                    conflict = (alo < hi and lo < ahi and
                                (aw or (t, lo, hi) in [x for x in wr]))
                    if conflict and not ordered(aj):
                        bad.append((aj, i, t))
                        if len(bad) > 50:
                            return bad
            for (t, lo, hi) in rd:
                acc.setdefault(t, []).append((lo, hi, False, i))
            for (t, lo, hi) in wr:
                acc.setdefault(t, []).append((lo, hi, True, i))
        return bad

    def _meta(self):
        n = len(self.seq)
        is_dma = [m in ("dma_start", "dma_start_transpose")
                  for (_, _, m, _, _, _) in self.seq]
        is_marker = [m == "__barrier__" for (_, _, m, _, _, _) in self.seq]
        dom = [("d_" + e) if is_dma[i] else ("c_" + e)
               for i, (e, _, m, _, _, _) in enumerate(self.seq)]
        return n, is_dma, is_marker, dom

    def _deps_pass(self):
        if getattr(self, "_deps_cache", None) is not None:
            return self._deps_cache
        n, is_dma, is_marker, dom = self._meta()
        acc = {}
        floor_dep = {}
        deps = [set() for _ in range(n)]
        last_in_dom = {}
        barrier_set = set()
        last_gp = {}
        last_q = {}
        serial_idx = set()
        for name_ in SERIAL_PHASES:
            parts = name_.split(":")
            lo, hi = self.phase_marks.get(parts[0], (0, 0))
            if len(parts) == 3:
                f0, f1 = float(parts[1]), float(parts[2])
                lo, hi = (lo + int((hi - lo) * f0)), (lo + int((hi - lo) * f1))
            serial_idx |= set(range(lo, hi))
        prev_real = None
        for i, (eng, _, meth, args, kw, inc) in enumerate(self.seq):
            if meth == "__barrier__":
                barrier_set = set(last_in_dom.values())
                continue
            wr, rd = self._rw(meth, args, kw)
            deps[i] |= barrier_set
            if i in serial_idx and prev_real is not None:
                deps[i].add(prev_real)
            prev_real = i
            if STRICT_GPSIMD and eng == "gpsimd":
                lp = last_gp.get("x")
                if lp is not None:
                    deps[i].add(lp)
                last_gp["x"] = i
            if STRICT_DMA and meth in ("dma_start", "dma_start_transpose"):
                lp = last_q.get(eng)
                if lp is not None:
                    deps[i].add(lp)
                last_q[eng] = i
            if meth == "collective_compute":
                deps[i] |= set(last_in_dom.values())
            for (t, lo, hi) in rd:
                for (alo, ahi, aw, aj) in acc.get(t, ()):
                    if aw and alo < hi and lo < ahi:
                        deps[i].add(aj)
                deps[i] |= floor_dep.get(t, set())
            for (t, lo, hi) in wr:
                for (alo, ahi, aw, aj) in acc.get(t, ()):
                    if alo < hi and lo < ahi:
                        deps[i].add(aj)
                deps[i] |= floor_dep.get(t, set())
            for (t, lo, hi) in rd:
                acc.setdefault(t, []).append((lo, hi, False, i))
            for (t, lo, hi) in wr:
                acc.setdefault(t, []).append((lo, hi, True, i))
            for (t, lo, hi) in wr + rd:
                lst = acc.get(t)
                if lst and len(lst) > 64:
                    drop, keep = lst[:32], lst[32:]
                    acc[t] = keep
                    fd = floor_dep.setdefault(t, set())
                    fd |= {j for (_, _, _, j) in drop}
                    if len(fd) > 16:
                        best = {}
                        for j in fd:
                            d_ = dom[j]
                            if d_ not in best or j > best[d_]:
                                best[d_] = j
                        floor_dep[t] = set(best.values())
            deps[i].discard(i)
            last_in_dom[dom[i]] = i
        marked = [False] * n
        for i in range(n):
            if is_dma[i]:
                marked[i] = True
            for j in deps[i]:
                marked[j] = True
        for i, (eng, _, meth, args, kw, inc) in enumerate(self.seq):
            if is_dma[i]:
                sp = self._span(kw["out"]) if meth == "dma_start" else \
                    self._span(args[0])
                if sp and sp[0] == "x_out":
                    marked[i] = True
        cnt = [0] * n
        run = {}
        for i in range(n):
            if marked[i]:
                run[dom[i]] = run.get(dom[i], 0) + 1
                cnt[i] = run[dom[i]]
        self._deps_cache = (deps, marked, cnt, dict(run))
        return self._deps_cache

    def emit(self, block, sems):
        n, is_dma, is_marker, dom = self._meta()
        deps, marked, cnt, run = self._deps_pass()
        per = {e: [] for e in self.ENGS}
        for i, (eng, _, meth, args, kw, inc) in enumerate(self.seq):
            if not is_marker[i]:
                per[eng].append(i)
        final_counts = dict(run)

        def unit(d):
            return 16 if d.startswith("d_") else 1

        for name in self.ENGS:
            idxs = per[name]
            if not idxs and name != "sync":
                continue
            dec = getattr(block, name)

            def body(e, idxs=idxs, name=name):
                vclock = {}
                for i in idxs:
                    (eng, _, meth, args, kw, inc) = self.seq[i]
                    want = {}
                    for j in sorted(deps[i]):
                        d_ = dom[j]
                        c_ = cnt[j]
                        if c_ > want.get(d_, 0):
                            want[d_] = c_
                    for d_, c_ in want.items():
                        if vclock.get(d_, 0) >= c_:
                            continue
                        e.wait_ge(sems[d_], c_ * unit(d_))
                        vclock[d_] = c_
                    try:
                        ins = getattr(e, meth)(*args, **kw)
                    except Exception as ex:
                        raise RuntimeError(
                            f"emit failed: {name}.{meth}") from ex
                    if marked[i]:
                        d_ = dom[i]
                        ins.then_inc(sems[d_], unit(d_))
                if name == "sync":
                    for d_, c_ in final_counts.items():
                        e.wait_ge(sems[d_], c_ * unit(d_))
            dec(body)


def emit_identity(b, ID):
    b.op("vector", "memset", ID[:], 1.0)
    b.op("gpsimd", "affine_select", ID[:], ID[:], pattern=[[-1, 128]],
         compare_op=ALU.is_ge, fill=0.0, base=0, channel_multiplier=1)
    b.op("gpsimd", "affine_select", ID[:], ID[:], pattern=[[1, 128]],
         compare_op=ALU.is_ge, fill=0.0, base=0, channel_multiplier=-1)


def emit_ln_tm(b, x_ap, out_ap, cols, D_, eps, sub_mean=True, sq_scratch=None):
    s1, s2, mcol, rstd = cols[:, 0:1], cols[:, 1:2], cols[:, 2:3], cols[:, 3:4]
    sq = sq_scratch if sq_scratch is not None else out_ap
    b.act(sq, x_ap, AF.Square, accum_out=s2)
    if sub_mean:
        b.op("vector", "reduce_sum", s1, x_ap, axis=AX.X)
        b.ts(mcol, s1, 1.0 / D_)
        b.ts(s2, s2, 1.0 / D_)
        b.tt(s1, mcol, mcol, ALU.mult)
        b.tt(s2, s2, s1, ALU.subtract)
    else:
        b.ts(s2, s2, 1.0 / D_)
    b.ts(s2, s2, eps, op0=ALU.add)
    b.act(s1, s2, AF.Sqrt)
    b.op("vector", "reciprocal", rstd, s1)
    if sub_mean:
        b.ts(out_ap, x_ap, mcol, rstd, op0=ALU.subtract, op1=ALU.mult)
    else:
        b.ts(out_ap, x_ap, rstd, op0=ALU.mult)


def _emit_sincos(b, src, K1, dst_sin, dst_cos, TMPA):
    for (dst, phase) in ((dst_sin, 0.0), (dst_cos, np.pi / 2)):
        if phase:
            b.ts(TMPA, src, float(np.float32(phase)), op0=ALU.add)
            s_ = TMPA
        else:
            s_ = src
        b.ts(K1, s_, INV2PI, RND, op0=ALU.mult, op1=ALU.add)
        b.ts(K1, K1, RND, op0=ALU.subtract)
        b.stt(dst, K1, -PI_HI, s_, op0=ALU.mult, op1=ALU.add)
        b.stt(dst, K1, -PI_LO, dst, op0=ALU.mult, op1=ALU.add)
        b.ts(dst, dst, PI_CLAMP, -PI_CLAMP, op0=ALU.min, op1=ALU.max)
        b.act(dst, dst, AF.Sin)


def emit_rope_apply(b, x1, x2, COS, SIN, T1, T2, T3):
    b.tt(T1, x1, COS, ALU.mult)
    b.tt(T2, x2, SIN, ALU.mult)
    b.tt(T1, T1, T2, ALU.subtract)
    b.tt(T2, x2, COS, ALU.mult)
    b.tt(T3, x1, SIN, ALU.mult)
    b.tt(x2, T2, T3, ALU.add)
    b.cp(x1, T1)


def emit_phase1(nc, b, G, ins, agql_i, agkv_i):
    with contextlib.ExitStack() as ctx:
        E = ctx.enter_context
        XT = E(nc.sbuf_tensor("XT1", [128, D], F32))
        LN1B = E(nc.sbuf_tensor("LN1B", [128, D], BF16))
        LN1F = E(nc.sbuf_tensor("LN1F", [128, D], BF16))
        WQA = E(nc.sbuf_tensor("WQA", [128, 16 * Q_LORA], BF16))
        WKVA = E(nc.sbuf_tensor("WKVA", [128, 16 * 576], BF16))
        QL = E(nc.sbuf_tensor("QL", [128, Q_LORA], F32))
        QLB = E(nc.sbuf_tensor("QLB", [128, Q_LORA], BF16))
        QLFO = E(nc.sbuf_tensor("QLFO", [128, 128], BF16))
        KV = E(nc.sbuf_tensor("KV1", [128, 640], F32))
        KVB = E(nc.sbuf_tensor("KVB1", [128, 640], BF16))
        KVFO = E(nc.sbuf_tensor("KVFO", [128, 128], BF16))
        QLNW = E(nc.sbuf_tensor("QLNW", [128, Q_LORA], F32))
        KVLNW = E(nc.sbuf_tensor("KVLNW", [128, KV_LORA], F32))
        TMP = E(nc.sbuf_tensor("TMP1", [128, D], F32))
        POSC = E(nc.sbuf_tensor("POSC", [128, 1], F32))
        IVB = E(nc.sbuf_tensor("IVB", [128, 32], F32))
        RSC = E(nc.sbuf_tensor("RSC", [128, 8 * 32], F32))
        PS0 = E(nc.psum_tensor("PS01", [128, 512], F32))
        PS2 = E(nc.psum_tensor("PS21", [128, 512], F32))
        PS1B = E(nc.psum_tensor("PS1B1", [128, 512], BF16))
        COLS = G["COLS"]

        F_, K1 = RSC[:, 0:32], RSC[:, 32:64]
        COS, SIN = RSC[:, 64:96], RSC[:, 96:128]
        T1, T2, T3 = RSC[:, 128:160], RSC[:, 160:192], RSC[:, 192:224]
        TMPA = RSC[:, 224:256]

        for kc in range(16):
            b.dma(WQA[:, kc * Q_LORA:(kc + 1) * Q_LORA],
                  ins["wqaT"][kc * 128:(kc + 1) * 128, :], eng="gpsimd")
            b.dma(WKVA[:, kc * 576:(kc + 1) * 576],
                  ins["wkvaT"][kc * 128:(kc + 1) * 128, :], eng="gpsimd")
        b.dma(TMP[0:1, 0:Q_LORA], ins["qlnw"][0:1, :], eng="gpsimd")
        b.bcast_rows(QLNW, TMP[0:1, 0:Q_LORA])
        b.dma(TMP[0:1, 0:KV_LORA], ins["kvlnw"][0:1, :], eng="gpsimd")
        b.bcast_rows(KVLNW, TMP[0:1, 0:KV_LORA])
        b.dma(TMP[0:1, 0:32], ins["invf"][0:1, :], eng="gpsimd")
        b.bcast_rows(IVB, TMP[0:1, 0:32])
        b.op("vector", "memset", KVB[:, 576:640], 0.0)

        for i in range(NT):
            rows = slice(i * 128, (i + 1) * 128)
            b.dma(XT[:, :], ins["x_in"][rows, :])
            b.dma(POSC[:, :], ins["pos"][rows, :])
            b.ts(F_, IVB, POSC[:, 0:1], op0=ALU.mult)
            _emit_sincos(b, F_, K1, SIN, COS, TMPA)
            emit_ln_tm(b, XT[:, :], LN1B, COLS, D, 1e-5, sub_mean=True,
                       sq_scratch=TMP)
            for kc in range(16):
                b.tp(PS1B[:, 0:128], LN1B[:, kc * 128:(kc + 1) * 128],
                     LN1F[:, kc * 128:(kc + 1) * 128], G["ID"][:])
            for nb in range(3):
                b.mm_acc(PS0[:, 0:512],
                         [(LN1F[:, kc * 128:(kc + 1) * 128],
                           WQA[:, kc * Q_LORA + nb * 512:
                               kc * Q_LORA + (nb + 1) * 512])
                          for kc in range(16)])
                b.cp(QL[:, nb * 512:(nb + 1) * 512], PS0[:, 0:512])
            b.act(TMP[:, 0:Q_LORA], QL[:, :], AF.Square,
                  accum_out=COLS[:, 1:2])
            b.ts(COLS[:, 1:2], COLS[:, 1:2], 1.0 / Q_LORA)
            b.ts(COLS[:, 1:2], COLS[:, 1:2], 1e-6, op0=ALU.add)
            b.act(COLS[:, 0:1], COLS[:, 1:2], AF.Sqrt)
            b.op("vector", "reciprocal", COLS[:, 3:4], COLS[:, 0:1])
            b.ts(QL[:, :], QL[:, :], COLS[:, 3:4], op0=ALU.mult)
            b.tt(QLB[:, :], QL[:, :], QLNW[:, :], ALU.mult)
            for kc in range(12):
                b.tp(PS1B[:, 0:128], QLB[:, kc * 128:(kc + 1) * 128],
                     QLFO[:, :], G["ID"][:])
                b.dma(agql_i[kc * 128:(kc + 1) * 128, rows], QLFO[:, :])
            b.mm_acc(PS0[:, 0:512],
                     [(LN1F[:, kc * 128:(kc + 1) * 128],
                       WKVA[:, kc * 576: kc * 576 + 512])
                      for kc in range(16)])
            b.cp(KV[:, 0:512], PS0[:, 0:512])
            b.mm_acc(PS2[:, 0:64],
                     [(LN1F[:, kc * 128:(kc + 1) * 128],
                       WKVA[:, kc * 576 + 512: kc * 576 + 576])
                      for kc in range(16)])
            b.cp(KV[:, 512:576], PS2[:, 0:64])
            b.act(TMP[:, 0:512], KV[:, 0:512], AF.Square,
                  accum_out=COLS[:, 1:2])
            b.ts(COLS[:, 1:2], COLS[:, 1:2], 1.0 / KV_LORA)
            b.ts(COLS[:, 1:2], COLS[:, 1:2], 1e-6, op0=ALU.add)
            b.act(COLS[:, 0:1], COLS[:, 1:2], AF.Sqrt)
            b.op("vector", "reciprocal", COLS[:, 3:4], COLS[:, 0:1])
            b.ts(KV[:, 0:512], KV[:, 0:512], COLS[:, 3:4], op0=ALU.mult)
            b.tt(KVB[:, 0:512], KV[:, 0:512], KVLNW[:, :], ALU.mult)
            emit_rope_apply(b, KV[:, 512:544], KV[:, 544:576], COS, SIN,
                            T1, T2, T3)
            b.cp(KVB[:, 512:576], KV[:, 512:576])
            for kc in range(5):
                b.tp(PS1B[:, 0:128], KVB[:, kc * 128:(kc + 1) * 128],
                     KVFO[:, :], G["ID"][:])
                b.dma(agkv_i[kc * 128:(kc + 1) * 128, rows], KVFO[:, :])


def emit_phase2(nc, b, G, ins, agql_o, agkv_o, xhf_d):
    with contextlib.ExitStack() as ctx:
        E = ctx.enter_context
        QN0 = E(nc.sbuf_tensor("QN0", [128, S], BF16))
        QN1 = E(nc.sbuf_tensor("QN1", [128, S], BF16))
        QX1 = E(nc.sbuf_tensor("QX1", [64, S], BF16))
        QX2 = E(nc.sbuf_tensor("QX2", [64, S], BF16))
        KPX1 = E(nc.sbuf_tensor("KPX1", [64, S], BF16))
        KPX2 = E(nc.sbuf_tensor("KPX2", [64, S], BF16))
        PB = E(nc.sbuf_tensor("PB", [128, S], BF16))
        PTB = E(nc.sbuf_tensor("PTB", [128, 128], BF16))
        PTBb = E(nc.sbuf_tensor("PTBb", [128, 128], BF16))
        MX = E(nc.sbuf_tensor("MX", [128, 16], F32))
        LC = E(nc.sbuf_tensor("LC", [128, 16], F32))
        MASK = E(nc.sbuf_tensor("MASK", [128, 4 * 512], BF16))
        XH = E(nc.sbuf_tensor("XH", [128, VH], F32))
        XHB = E(nc.sbuf_tensor("XHB", [128, VH], BF16))
        PSS = E(nc.psum_tensor("PSS", [128, 512], F32))
        PSSb = E(nc.psum_tensor("PSSb", [128, 512], F32))
        PSK = E(nc.psum_tensor("PSK2", [128, 512], F32))
        PSV = E(nc.psum_tensor("PSV2", [128, 512], F32))
        PSO = E(nc.psum_tensor("PSO2", [128, 512], F32))
        PS1B = E(nc.psum_tensor("PS1B2", [128, 512], BF16))
        PS1Bb = E(nc.psum_tensor("PS1B2b", [128, 512], BF16))
        sub = ctx.enter_context(contextlib.ExitStack())
        E2 = sub.enter_context
        WQBS = E2(nc.sbuf_tensor("WQBS", [128, 12 * 384], BF16))
        QLS = E2(nc.sbuf_tensor("QLS", [128, 12 * 512], BF16))
        IVC = E2(nc.sbuf_tensor("IVC", [64, 1], F32))
        RSCF = E2(nc.sbuf_tensor("RSCF", [64, 6 * 512], F32))
        PSQP = E2(nc.psum_tensor("PSQP", [128, 512], F32))
        COLS = G["COLS"]

        for qq in range(4):
            m = MASK[:, qq * 512:(qq + 1) * 512]
            b.op("vector", "memset", m, 0.0)
            b.op("gpsimd", "affine_select", m, m, pattern=[[-1, 512]],
                 compare_op=ALU.is_ge, fill=-30000.0, base=qq * 128,
                 channel_multiplier=1)
        b.dma(IVC[0:64, :], ins["invf64"][:, :], eng="gpsimd")
        b.dma(WQBS[:, :], ins["wqbTs"][:, :], eng="gpsimd")
        for p in range(NCORES):
            tc = slice(p * TS, (p + 1) * TS)
            b.dma(KPX1[0:32, tc], agkv_o[p * 640 + 512: p * 640 + 544, :])
            b.dma(KPX1[32:64, tc], agkv_o[p * 640 + 512: p * 640 + 544, :])
            b.dma(KPX2[0:32, tc], agkv_o[p * 640 + 544: p * 640 + 576, :])
            b.dma(KPX2[32:64, tc], agkv_o[p * 640 + 544: p * 640 + 576, :])
        POS = RSCF[0:64, 0:512]
        FR = RSCF[0:64, 512:1024]
        KR = RSCF[0:64, 1024:1536]
        CSF = RSCF[0:64, 1536:2048]
        SNF = RSCF[0:64, 2048:2560]
        TMR = RSCF[0:64, 2560:3072]
        for tb in range(16):
            p, t2 = tb // 2, tb % 2
            gcols = slice(tb * 512, (tb + 1) * 512)
            for kc in range(12):
                b.dma(QLS[:, kc * 512:(kc + 1) * 512],
                      agql_o[p * 1536 + kc * 128: p * 1536 + (kc + 1) * 128,
                             t2 * 512:(t2 + 1) * 512])
            for (oc, (off, sz)) in enumerate(((0, 128), (128, 128),
                                              (256, 64), (320, 64))):
                b.mm_acc(PSQP[0:sz, 0:512],
                         [(WQBS[:, kc * 384 + off: kc * 384 + off + sz],
                           QLS[:, kc * 512:(kc + 1) * 512])
                          for kc in range(12)])
                dst = (QN0, QN1, QX1, QX2)[oc]
                b.ts(dst[0:sz, gcols], PSQP[0:sz, 0:512], SCALE, op0=ALU.mult)
            b.op("gpsimd", "iota", POS, pattern=[[1, 512]], base=tb * 512,
                 channel_multiplier=0, allow_small_or_imprecise_dtypes=True)
            b.ts(FR, POS, IVC[0:64, 0:1], op0=ALU.mult)
            _emit_sincos(b, FR, KR, SNF, CSF, TMR)
            x1, x2 = QX1[0:64, gcols], QX2[0:64, gcols]
            T1, T2 = RSCF[0:64, 0:512], RSCF[0:64, 512:1024]
            T3 = RSCF[0:64, 2560:3072]
            b.tt(T1, x1, CSF, ALU.mult)
            b.tt(T2, x2, SNF, ALU.mult)
            b.tt(T1, T1, T2, ALU.subtract)
            b.tt(T2, x2, CSF, ALU.mult)
            b.tt(T3, x1, SNF, ALU.mult)
            b.tt(x2, T2, T3, ALU.add)
            b.cp(x1, T1)

        sub.close()
        b.barrier()
        KH = E(nc.sbuf_tensor("KH", [128, S], BF16))
        VHT = E(nc.sbuf_tensor("VHT", [128, S], BF16))
        SROW = E(nc.sbuf_tensor("SROW", [128, S], F32))
        QT = E(nc.sbuf_tensor("QT2", [128, 512], BF16))
        OT = E(nc.sbuf_tensor("OT2", [128, 512], BF16))
        KVF2 = E(nc.sbuf_tensor("KVF2", [128, 512], BF16))
        for hh in range(HPC):
            b.dma(QT[:, :], ins["qabsT"][hh * 128:(hh + 1) * 128, :],
                  eng="gpsimd")
            b.dma(OT[:, :], ins["oaT"][hh * 128:(hh + 1) * 128, :],
                  eng="gpsimd")
            QNF = QN0 if hh == 0 else QN1
            for tb in range(16):
                p, t2 = tb // 2, tb % 2
                gcols = slice(tb * 512, (tb + 1) * 512)
                for cc in range(4):
                    b.dma(KVF2[:, 0:512],
                          agkv_o[p * 640 + cc * 128: p * 640 + (cc + 1) * 128,
                                 t2 * 512:(t2 + 1) * 512])
                    b.mm(PSK[:, 0:512], QT[:, cc * 128:(cc + 1) * 128],
                         KVF2[:, 0:512], start=(cc == 0), stop=(cc == 3))
                    b.mm(PSV[:, 0:512], OT[:, cc * 128:(cc + 1) * 128],
                         KVF2[:, 0:512], start=(cc == 0), stop=(cc == 3))
                b.cp(KH[:, gcols], PSK[:, 0:512])
                b.cp(PB[:, 0:512], PSV[:, 0:512])
                for j in range(4):
                    ct = tb * 4 + j
                    b.tp(PS1B[:, 0:128], PB[:, j * 128:(j + 1) * 128],
                         VHT[:, ct * 128:(ct + 1) * 128], G["ID"][:])
            for qi in range(NTF):
                nk = qi // 4 + 1
                qcols = slice(qi * 128, (qi + 1) * 128)
                h32 = slice(hh * 32, (hh + 1) * 32)
                for kb in range(nk):
                    kcols = slice(kb * 512, (kb + 1) * 512)
                    PSx = PSS if kb % 2 == 0 else PSSb
                    b.mm(PSx[:, 0:512], QNF[:, qcols], KH[:, kcols],
                         start=True, stop=False)
                    b.mm(PSx[:, 0:512], QX1[h32, qcols], KPX1[h32, kcols],
                         start=False, stop=False)
                    b.mm(PSx[:, 0:512], QX2[h32, qcols], KPX2[h32, kcols],
                         start=False, stop=True)
                    if kb == nk - 1:
                        b.tt(PSx[:, 0:512], PSx[:, 0:512],
                             MASK[:, (qi % 4) * 512:(qi % 4 + 1) * 512],
                             ALU.add)
                    b.op("vector", "reduce_max", MX[:, kb:kb + 1],
                         PSx[:, 0:512], axis=AX.X)
                    b.cp(SROW[:, kcols], PSx[:, 0:512])
                b.op("vector", "reduce_max", COLS[:, 0:1], MX[:, 0:nk],
                     axis=AX.X, negate=True)
                for kb in range(nk):
                    kcols = slice(kb * 512, (kb + 1) * 512)
                    b.act(PB[:, kcols], SROW[:, kcols], AF.Exp,
                          bias=COLS[:, 0:1], accum_out=LC[:, kb:kb + 1])
                b.op("vector", "reduce_sum", COLS[:, 1:2], LC[:, 0:nk],
                     axis=AX.X)
                b.op("vector", "reciprocal", COLS[:, 2:3], COLS[:, 1:2])
                nch = nk * 4
                for ct in range(nch):
                    PSt = PS1B if ct % 2 == 0 else PS1Bb
                    PTx = PTB if ct % 2 == 0 else PTBb
                    b.op("tensor", "transpose", PSt[:, 0:128],
                         PB[:, ct * 128:(ct + 1) * 128], G["ID"][:])
                    b.cp(PTx[:, :], PSt[:, 0:128])
                    b.mm(PSO[:, 0:VH], PTx[:, :],
                         VHT[:, ct * 128:(ct + 1) * 128],
                         start=(ct == 0), stop=(ct == nch - 1))
                b.ts(XH[:, :], PSO[:, 0:VH], COLS[:, 2:3], op0=ALU.mult)
                b.cp(XHB[:, :], XH[:, :])
                b.tp(PS1B[:, 0:128], XHB[:, :], PTB[:, :], G["ID"][:])
                b.dma(xhf_d[hh * 128:(hh + 1) * 128, qcols], PTB[:, :])


def emit_partial_proj(nc, b, G, tag, nchunk, src_d, wTs_in, rs_in):
    with contextlib.ExitStack() as ctx:
        E = ctx.enter_context
        XF = E(nc.sbuf_tensor(f"XF{tag}", [128, nchunk * 128], BF16))
        WS = E(nc.sbuf_tensor(f"WS{tag}", [128, nchunk * D], BF16))
        RO = E(nc.sbuf_tensor(f"RO{tag}", [128, D], BF16))
        PSN = [E(nc.psum_tensor(f"PSN{tag}{j}", [128, 512], F32))
               for j in range(4)]
        for ch in range(nchunk):
            b.dma(WS[:, ch * D:(ch + 1) * D],
                  wTs_in[ch * 128:(ch + 1) * 128, :], eng="gpsimd")
        for i in range(NTF):
            tcols = slice(i * 128, (i + 1) * 128)
            for ch in range(nchunk):
                b.dma(XF[:, ch * 128:(ch + 1) * 128],
                      src_d[ch * 128:(ch + 1) * 128, tcols])
            for nb in range(4):
                for ch in range(nchunk):
                    b.mm(PSN[nb][:, 0:512], XF[:, ch * 128:(ch + 1) * 128],
                         WS[:, ch * D + nb * 512: ch * D + (nb + 1) * 512],
                         start=(ch == 0), stop=(ch == nchunk - 1))
            for nb in range(4):
                b.cp(RO[:, nb * 512:(nb + 1) * 512], PSN[nb][:, 0:512])
            b.dma(rs_in[i * 128:(i + 1) * 128, :], RO[:, :])


def emit_resid_ffn(nc, b, G, tag, ins, rs_out, x_src, x_dst, ln_dst):
    with contextlib.ExitStack() as ctx:
        E = ctx.enter_context
        FWT = E(nc.sbuf_tensor(f"FWT{tag}", [128, 16 * D], BF16))
        GB = E(nc.sbuf_tensor(f"GB{tag}", [128, D], F32))
        BB = E(nc.sbuf_tensor(f"BB{tag}", [128, D], F32))
        TMPP = [E(nc.sbuf_tensor(f"TMP{tag}" + str(_p), [128, D], F32))
                for _p in range(2)]
        TMP = TMPP[0]
        TMP2P = [E(nc.sbuf_tensor(f"TMP2{tag}" + str(_p), [128, D], F32))
                for _p in range(2)]
        TMP2 = TMP2P[0]
        XQP = [E(nc.sbuf_tensor(f"XQ{tag}" + str(_p), [128, D], F32))
                for _p in range(2)]
        XQ = XQP[0]
        XQTP = [E(nc.sbuf_tensor(f"XQT{tag}" + str(_p), [128, D], BF16))
                for _p in range(2)]
        XQT = XQTP[0]
        RBP = [E(nc.sbuf_tensor(f"RB{tag}" + str(_p), [128, D], BF16))
                for _p in range(2)]
        RB = RBP[0]
        GCOL = E(nc.sbuf_tensor(f"GCOL{tag}", [128, 16], F32))
        LNSP = [E(nc.sbuf_tensor(f"LNS{tag}" + str(_p), [128, 128], BF16))
                for _p in range(2)]
        LNS = LNSP[0]
        XT2P = [E(nc.sbuf_tensor(f"XT2{tag}" + str(_p), [128, D], F32))
                for _p in range(2)]
        XT2 = XT2P[0]
        PS0P = [E(nc.psum_tensor(f"PS0{tag}" + str(_p), [128, 512], F32))
                for _p in range(2)]
        PS0 = PS0P[0]
        PS1P = [E(nc.psum_tensor(f"PS1{tag}" + str(_p), [128, 512], F32))
                for _p in range(2)]
        PS1 = PS1P[0]
        PS1BP = [E(nc.psum_tensor(f"PS1B{tag}" + str(_p), [128, 512],
                 BF16)) for _p in range(2)]
        PS1B = PS1BP[0]
        COLS = G["COLS"]

        for i in range(16):
            b.dma(TMP[:, :], ins["fw"][i * 128:(i + 1) * 128, :])
            b.op("vector", "reduce_sum", GCOL[:, i:i + 1], TMP[:], axis=AX.X,
                 apply_absolute_value=True)
        b.ts(GCOL[:], GCOL[:], 1.0 / D)
        for i in range(16):
            b.dma(TMP[0:1, i * 128:(i + 1) * 128], GCOL[:, i:i + 1],
                  eng="gpsimd")
        b.bcast_rows(GB, TMP[0:1, 0:D])
        b.dma(TMP[0:1, 0:D], ins["fb"][0:1, :], eng="gpsimd")
        b.bcast_rows(BB, TMP[0:1, 0:D])
        b.ts(TMP, GB, 1e-8, op0=ALU.add)
        b.op("vector", "reciprocal", TMP, TMP)
        for i in range(16):
            b.dma(TMP2[:, :], ins["fwT"][i * 128:(i + 1) * 128, :])
            b.tt(TMP2[:], TMP2[:], TMP[:], ALU.mult)
            b.ts(TMP2[:], TMP2[:], RND, RND, op0=ALU.add, op1=ALU.subtract)
            b.ts(TMP2[:], TMP2[:], 1.0, -1.0, op0=ALU.min, op1=ALU.max)
            b.cp(FWT[:, i * D:(i + 1) * D], TMP2[:])

        for i in range(NT):
            TMP, TMP2, XQ, XQT, RB, XT2, LNS = (
                TMPP[i % 2], TMP2P[i % 2], XQP[i % 2], XQTP[i % 2],
                RBP[i % 2], XT2P[i % 2], LNSP[i % 2])
            PS0, PS1, PS1B = PS0P[i % 2], PS1P[i % 2], PS1BP[i % 2]
            xt = XT2[:, :]
            b.dma(xt, x_src[i * 128:(i + 1) * 128, :])
            b.dma(RB[:, :], rs_out[i * 128:(i + 1) * 128, :])
            b.tt(xt, xt, RB[:, :], ALU.add)
            emit_ln_tm(b, xt, TMP2, COLS, D, 1e-5, sub_mean=True,
                       sq_scratch=TMP)
            emit_ln_tm(b, TMP2[:, 0:D], TMP, COLS, D, 1e-8, sub_mean=False)
            b.op("vector", "reduce_max", COLS[:, 4:5], TMP, axis=AX.X,
                 apply_absolute_value=True)
            b.ts(COLS[:, 4:5], COLS[:, 4:5], 1.0 / 127.0)
            b.ts(COLS[:, 5:6], COLS[:, 4:5], 1e-8, op0=ALU.add)
            b.op("vector", "reciprocal", COLS[:, 5:6], COLS[:, 5:6])
            b.ts(XQ, TMP, COLS[:, 5:6], RND, op0=ALU.mult, op1=ALU.add)
            b.ts(XQ, XQ, RND, 127.0, op0=ALU.subtract, op1=ALU.min)
            b.ts(XQ, XQ, -127.0, op0=ALU.max)
            for kc in range(16):
                b.tp(PS1[:, 0:128], XQ[:, kc * 128:(kc + 1) * 128],
                     XQT[:, kc * 128:(kc + 1) * 128], G["IDF"][:])
            for nb in range(D // 512):
                b.mm_acc(PS0[:, 0:512],
                         [(XQT[:, kc * 128:(kc + 1) * 128],
                           FWT[:, kc * D + nb * 512: kc * D + (nb + 1) * 512])
                          for kc in range(16)])
                sl = slice(nb * 512, (nb + 1) * 512)
                b.stt(TMP[:, sl], PS0[:, 0:512], COLS[:, 4:5], GB[:, sl],
                      op0=ALU.mult, op1=ALU.mult)
                b.tt(TMP[:, sl], TMP[:, sl], BB[:, sl], ALU.add)
                b.tt(xt[:, sl], xt[:, sl], TMP[:, sl], ALU.add)
            b.dma(x_dst[i * 128:(i + 1) * 128, :], xt)
            if ln_dst is not None:
                emit_ln_tm(b, xt, TMP2[:, 0:D], COLS, D, 1e-5, sub_mean=True,
                           sq_scratch=TMP)
                b.cp(XQT[:, 0:D], TMP2[:, 0:D])
                for kc in range(16):
                    b.op("tensor", "transpose", PS1B[:, 0:128],
                         XQT[:, kc * 128:(kc + 1) * 128], G["ID"][:])
                    b.cp(LNS[:, :], PS1B[:, 0:128])
                    b.dma(ln_dst[kc * 128:(kc + 1) * 128,
                                 i * 128:(i + 1) * 128], LNS[:, :])


def emit_phase4(nc, b, G, ins, aglnx_o, obf_d):
    SC = 1024
    NSC = S // SC
    ID, IDF, COLS = G["ID"], G["IDF"], G["COLS"]
    with contextlib.ExitStack() as ctx:
        E = ctx.enter_context
        WQ = E(nc.sbuf_tensor("WQ4", [128, 16 * 1544], BF16))
        LNXF = E(nc.sbuf_tensor("LNXF", [128, 16 * 512], BF16))
        CONVF = E(nc.sbuf_tensor("CONVF", [128, 8 * SC], F32))
        ZF = E(nc.sbuf_tensor("ZF", [128, 4 * SC], BF16))
        BAF = E(nc.sbuf_tensor("BAF", [8, SC], F32))
        XEXT = E(nc.sbuf_tensor("XEXT", [128, SC + 3], F32))
        CARRY = E(nc.sbuf_tensor("CARRY", [128, 8 * 3], F32))
        CONVW = E(nc.sbuf_tensor("CONVW", [128, 32], F32))
        QKVTM = E(nc.sbuf_tensor("QKVTM", [128, 8 * 128], F32))
        ZTM = E(nc.sbuf_tensor("ZTM", [128, 4 * 128], BF16))
        BATM = E(nc.sbuf_tensor("BATM", [128, 8], F32))
        QNB = E(nc.sbuf_tensor("QNB", [128, 2 * 128], BF16))
        KNB = E(nc.sbuf_tensor("KNB", [128, 2 * 128], BF16))
        VB = E(nc.sbuf_tensor("VB", [128, 4 * 128], BF16))
        G4 = E(nc.sbuf_tensor("G4", [128, 4], F32))
        BETA4 = E(nc.sbuf_tensor("BETA4", [128, 4], F32))
        NEGB4 = E(nc.sbuf_tensor("NEGB4", [128, 4], F32))
        SF = E(nc.sbuf_tensor("SF", [128, 4 * 128], F32))
        SB = E(nc.sbuf_tensor("SB", [128, 4 * 128], BF16))
        MKK = E(nc.sbuf_tensor("MKK", [128, 128], F32))
        GRAW = E(nc.sbuf_tensor("GRAW", [128, 128], F32))
        ES = E(nc.sbuf_tensor("ES", [128, 128], F32))
        ETI = E(nc.sbuf_tensor("ETI", [128, 128], F32))
        NMAT = E(nc.sbuf_tensor("NMAT", [128, 128], BF16))
        NTMAT = E(nc.sbuf_tensor("NTMAT", [128, 128], BF16))
        NP2 = E(nc.sbuf_tensor("NP2", [128, 128], BF16))
        NTP2 = E(nc.sbuf_tensor("NTP2", [128, 128], BF16))
        UB = E(nc.sbuf_tensor("UB", [128, 128], BF16))
        VBETA = E(nc.sbuf_tensor("VBETA", [128, 128], F32))
        KF = E(nc.sbuf_tensor("KF", [128, 128], BF16))
        QF = E(nc.sbuf_tensor("QF", [128, 128], BF16))
        KSC_ = E(nc.sbuf_tensor("KSC", [128, 128], BF16))
        KSF = E(nc.sbuf_tensor("KSF", [128, 128], BF16))
        KRB = E(nc.sbuf_tensor("KRB", [128, 128], BF16))
        GTB = E(nc.sbuf_tensor("GTB", [128, 128], BF16))
        OFF = E(nc.sbuf_tensor("OFF", [128, 128], F32))
        OG = E(nc.sbuf_tensor("OG", [128, 128], BF16))
        GROW = E(nc.sbuf_tensor("GROW", [1, 128], F32))
        SROW1 = E(nc.sbuf_tensor("SROW1", [1, 128], F32))
        SCOL = E(nc.sbuf_tensor("SCOL", [128, 1], F32))
        NSCOL = E(nc.sbuf_tensor("NSCOL", [128, 1], F32))
        BSC = E(nc.sbuf_tensor("BSC", [128, 1], F32))
        BCOL = E(nc.sbuf_tensor("BCOL", [128, 1], F32))
        RCOL = E(nc.sbuf_tensor("RCOL", [128, 1], F32))
        ONES1 = E(nc.sbuf_tensor("ONES1", [1, 128], F32))
        C1R = E(nc.sbuf_tensor("C1R", [1, 1], F32))
        E127 = E(nc.sbuf_tensor("E127", [128, 128], F32))
        MADDA = E(nc.sbuf_tensor("MADDA", [128, 128], F32))
        MADDB = E(nc.sbuf_tensor("MADDB", [128, 128], F32))
        NEGA = E(nc.sbuf_tensor("NEGA", [128, 4], F32))
        DTB = E(nc.sbuf_tensor("DTB", [128, 4], F32))
        NWB = E(nc.sbuf_tensor("NWB", [128, 128], F32))
        NMATP = [E(nc.sbuf_tensor("NMAT_p" + str(_p), [128, 128], BF16)) for _p in range(2)]
        NTMATP = [E(nc.sbuf_tensor("NTMAT_p" + str(_p), [128, 128], BF16)) for _p in range(2)]
        NP2P = [E(nc.sbuf_tensor("NP2_p" + str(_p), [128, 128], BF16)) for _p in range(2)]
        NTP2P = [E(nc.sbuf_tensor("NTP2_p" + str(_p), [128, 128], BF16)) for _p in range(2)]
        UBP = [E(nc.sbuf_tensor("UB_p" + str(_p), [128, 128], BF16)) for _p in range(2)]
        VBETAP = [E(nc.sbuf_tensor("VBETA_p" + str(_p), [128, 128], F32)) for _p in range(2)]
        KSC_P = [E(nc.sbuf_tensor("KSC__p" + str(_p), [128, 128], BF16)) for _p in range(2)]
        KSFP = [E(nc.sbuf_tensor("KSF_p" + str(_p), [128, 128], BF16)) for _p in range(2)]
        KRBP = [E(nc.sbuf_tensor("KRB_p" + str(_p), [128, 128], BF16)) for _p in range(2)]
        GTBP = [E(nc.sbuf_tensor("GTB_p" + str(_p), [128, 128], BF16)) for _p in range(2)]
        OFFP = [E(nc.sbuf_tensor("OFF_p" + str(_p), [128, 128], F32)) for _p in range(2)]
        OGP = [E(nc.sbuf_tensor("OG_p" + str(_p), [128, 128], BF16)) for _p in range(2)]
        GROWP = [E(nc.sbuf_tensor("GROW_p" + str(_p), [1, 128], F32)) for _p in range(2)]
        SROW1P = [E(nc.sbuf_tensor("SROW1_p" + str(_p), [1, 128], F32)) for _p in range(2)]
        SCOLP = [E(nc.sbuf_tensor("SCOL_p" + str(_p), [128, 1], F32)) for _p in range(2)]
        NSCOLP = [E(nc.sbuf_tensor("NSCOL_p" + str(_p), [128, 1], F32)) for _p in range(2)]
        BSCP = [E(nc.sbuf_tensor("BSC_p" + str(_p), [128, 1], F32)) for _p in range(2)]
        BCOLP = [E(nc.sbuf_tensor("BCOL_p" + str(_p), [128, 1], F32)) for _p in range(2)]
        RCOLP = [E(nc.sbuf_tensor("RCOL_p" + str(_p), [128, 1], F32)) for _p in range(2)]
        ESP = [E(nc.sbuf_tensor("ES_p" + str(_p), [128, 128], F32)) for _p in range(2)]
        ETIP = [E(nc.sbuf_tensor("ETI_p" + str(_p), [128, 128], F32)) for _p in range(2)]
        KFK = [E(nc.sbuf_tensor("KF_k" + str(_p), [128, 128], BF16)) for _p in range(2)]
        QFK = [E(nc.sbuf_tensor("QF_k" + str(_p), [128, 128], BF16)) for _p in range(2)]
        MKKK = [E(nc.sbuf_tensor("MKK_k" + str(_p), [128, 128], F32)) for _p in range(2)]
        GRAWK = [E(nc.sbuf_tensor("GRAW_k" + str(_p), [128, 128], F32)) for _p in range(2)]

        PSQ = E(nc.psum_tensor("PSQ", [128, 512], F32))
        PSOU = E(nc.psum_tensor("PSOU", [128, 512], F32))
        PSA = E(nc.psum_tensor("PSA4", [128, 512], F32))
        PSM = E(nc.psum_tensor("PSM", [128, 512], F32))
        PSM2 = E(nc.psum_tensor("PSM2", [128, 512], F32))
        PS1 = E(nc.psum_tensor("PS14", [128, 512], F32))
        PS1B = E(nc.psum_tensor("PS1B4", [128, 512], BF16))

        b.op("vector", "memset", ONES1[0:1, :], 1.0)
        b.op("vector", "memset", C1R[0:1, :], 1.0)
        b.op("vector", "memset", E127[:], 1.0)
        b.op("gpsimd", "affine_select", E127[:], E127[:], pattern=[[0, 128]],
             compare_op=ALU.is_ge, fill=0.0, base=-127, channel_multiplier=1)
        b.op("vector", "memset", MADDA[:], 0.0)
        b.op("gpsimd", "affine_select", MADDA[:], MADDA[:],
             pattern=[[-1, 128]], compare_op=ALU.is_ge, fill=1e4, base=-1,
             channel_multiplier=1)
        b.op("vector", "memset", MADDB[:], 0.0)
        b.op("gpsimd", "affine_select", MADDB[:], MADDB[:],
             pattern=[[1, 128]], compare_op=ALU.is_ge, fill=-1e4, base=0,
             channel_multiplier=-1)
        b.op("vector", "memset", CARRY[:], 0.0)
        b.op("vector", "memset", SF[:], 0.0)
        b.op("vector", "memset", SB[:], 0.0)
        for kc in range(16):
            b.dma(WQ[:, kc * 1544:(kc + 1) * 1544],
                  ins["wq"][kc * 128:(kc + 1) * 128, :], eng="gpsimd")
        b.dma(CONVW[:, :], ins["convw"][:, :])
        b.dma(NEGA[0:1, :], ins["alog"][0:1, :])
        b.act(NEGA[0:1, :], NEGA[0:1, :], AF.Exp)
        b.ts(NEGA[0:1, :], NEGA[0:1, :], -1.0)
        b.bcast_rows(NEGA, NEGA[0:1, :])
        b.dma(DTB[0:1, :], ins["dtb"][0:1, :])
        b.bcast_rows(DTB, DTB[0:1, :])
        b.dma(NWB[0:1, :], ins["normw"][0:1, :])
        b.bcast_rows(NWB, NWB[0:1, :])

        for sc in range(NSC):
            for t2 in range(2):
                for kc in range(16):
                    b.dma(LNXF[:, kc * 512:(kc + 1) * 512],
                          aglnx_o[sc * D + kc * 128: sc * D + (kc + 1) * 128,
                                  t2 * 512:(t2 + 1) * 512])
                for cc in range(12):
                    b.mm_acc(PSQ[:, 0:512],
                             [(WQ[:, kc * 1544 + cc * 128:
                                  kc * 1544 + (cc + 1) * 128],
                               LNXF[:, kc * 512:(kc + 1) * 512])
                              for kc in range(16)])
                    if cc < 8:
                        b.cp(CONVF[:, cc * SC + t2 * 512:
                                   cc * SC + (t2 + 1) * 512], PSQ[:, 0:512])
                    else:
                        zc = cc - 8
                        b.act(ZF[:, zc * SC + t2 * 512:
                                 zc * SC + (t2 + 1) * 512], PSQ[:, 0:512],
                              AF.Silu)
                b.mm_acc(PSQ[0:8, 0:512],
                         [(WQ[:, kc * 1544 + 1536: kc * 1544 + 1544],
                           LNXF[:, kc * 512:(kc + 1) * 512])
                          for kc in range(16)])
                b.cp(BAF[0:8, t2 * 512:(t2 + 1) * 512], PSQ[0:8, 0:512])
            for chm in range(8):
                b.cp(XEXT[:, 0:3], CARRY[:, chm * 3:(chm + 1) * 3])
                b.cp(XEXT[:, 3:SC + 3], CONVF[:, chm * SC:(chm + 1) * SC])
                b.cp(CARRY[:, chm * 3:(chm + 1) * 3], XEXT[:, SC:SC + 3])
                dst = CONVF[:, chm * SC:(chm + 1) * SC]
                b.ts(dst, XEXT[:, 0:SC], CONVW[:, chm * 4:chm * 4 + 1],
                     op0=ALU.mult)
                for j in range(1, 4):
                    b.stt(dst, XEXT[:, j:j + SC],
                          CONVW[:, chm * 4 + j:chm * 4 + j + 1], dst,
                          op0=ALU.mult, op1=ALU.add)
                b.act(dst, dst, AF.Silu)

            for ti in range(8):
                ck = sc * 8 + ti
                rows = slice(ck * 128, (ck + 1) * 128)
                for chm in range(8):
                    b.tp(PS1[:, 0:128],
                         CONVF[:, chm * SC + ti * 128:
                               chm * SC + (ti + 1) * 128],
                         QKVTM[:, chm * 128:(chm + 1) * 128], IDF[:])
                for vh in range(4):
                    b.tp(PS1B[:, 0:128],
                         ZF[:, vh * SC + ti * 128: vh * SC + (ti + 1) * 128],
                         ZTM[:, vh * 128:(vh + 1) * 128], ID[:])
                b.op("tensor", "transpose", PS1[:, 0:8],
                     BAF[0:8, ti * 128:(ti + 1) * 128], IDF[0:8, 0:8])
                b.cp(BATM[:, 0:8], PS1[:, 0:8])
                for j in range(4):
                    sl = QKVTM[:, j * 128:(j + 1) * 128]
                    b.act(OFF[:, :], sl, AF.Square, accum_out=COLS[:, 0:1])
                    b.ts(COLS[:, 0:1], COLS[:, 0:1], 1e-6, op0=ALU.add)
                    b.act(COLS[:, 1:2], COLS[:, 0:1], AF.Sqrt)
                    b.op("vector", "reciprocal", COLS[:, 2:3], COLS[:, 1:2])
                    if j < 2:
                        b.ts(QNB[:, j * 128:(j + 1) * 128], sl, COLS[:, 2:3],
                             float(DK ** -0.5), op0=ALU.mult, op1=ALU.mult)
                    else:
                        b.ts(KNB[:, (j - 2) * 128:(j - 1) * 128], sl,
                             COLS[:, 2:3], op0=ALU.mult)
                for vh in range(4):
                    b.cp(VB[:, vh * 128:(vh + 1) * 128],
                         QKVTM[:, (4 + vh) * 128:(5 + vh) * 128])
                for vh in range(4):
                    kh = vh // 2
                    bcol = BATM[:, kh * 4 + vh % 2: kh * 4 + vh % 2 + 1]
                    acol = BATM[:, kh * 4 + 2 + vh % 2: kh * 4 + 3 + vh % 2]
                    t_ = COLS[:, 3:4]
                    b.tt(t_, acol, DTB[:, vh:vh + 1], ALU.add)
                    b.act(COLS[:, 4:5], t_, AF.Abs)
                    b.act(COLS[:, 4:5], COLS[:, 4:5], AF.Exp, scale=-1.0)
                    b.act(COLS[:, 4:5], COLS[:, 4:5], AF.Ln, bias=1.0)
                    b.ts(COLS[:, 5:6], t_, 0.0, op0=ALU.max)
                    b.tt(COLS[:, 4:5], COLS[:, 4:5], COLS[:, 5:6], ALU.add)
                    b.tt(G4[:, vh:vh + 1], COLS[:, 4:5], NEGA[:, vh:vh + 1],
                         ALU.mult)
                    b.act(BETA4[:, vh:vh + 1], bcol, AF.Sigmoid)
                b.ts(NEGB4[:], BETA4[:], -1.0)

                for vh in range(4):
                    kh = vh // 2
                    pp = vh % 2
                    ktm = KNB[:, kh * 128:(kh + 1) * 128]
                    qtm = QNB[:, kh * 128:(kh + 1) * 128]
                    if vh % 2 == 0:
                        b.tp(PS1B[:, 0:128], ktm, KFK[kh % 2][:, :], ID[:])
                        b.tp(PS1B[:, 0:128], qtm, QFK[kh % 2][:, :], ID[:])
                        b.mm(PSM[:, 0:128], KFK[kh % 2][:, :], KFK[kh % 2][:, :], True, True)
                        b.cp(MKKK[kh % 2][:, :], PSM[:, 0:128])
                        b.mm(PSM[:, 0:128], KFK[kh % 2][:, :], QFK[kh % 2][:, :], True, True)
                        b.cp(GRAWK[kh % 2][:, :], PSM[:, 0:128])
                    b.op("tensor", "transpose", PS1[0:1, 0:128],
                         G4[:, vh:vh + 1], IDF[:])
                    b.cp(GROWP[pp][0:1, :], PS1[0:1, 0:128])
                    b.op("vector", "tensor_tensor_scan", SROW1P[pp][0:1, :],
                         GROWP[pp][0:1, :], GROWP[pp][0:1, :], 0.0, op0=ALU.add,
                         op1=ALU.bypass)
                    b.mm(PSOU[:, 0:1], SROW1P[pp][0:1, :], C1R[0:1, 0:1], True,
                         True)
                    b.cp(SCOLP[pp][:, :], PSOU[:, 0:1])
                    b.ts(NSCOLP[pp][:], SCOLP[pp][:], -1.0)
                    b.mm(PSOU[:, 0:1], E127[:, :], SCOLP[pp][:, :], True, True)
                    b.cp(BSCP[pp][:, :], PSOU[:, 0:1])
                    b.act(BCOLP[pp][:, :], BSCP[pp][:, :], AF.Exp)
                    b.act(RCOLP[pp][:, :], SCOLP[pp][:, :], AF.Exp, scale=-1.0,
                          bias=BSCP[pp][:, 0:1])
                    b.mm(PSOU[:, 0:128], ONES1[0:1, :], SROW1P[pp][0:1, :], True,
                         False)
                    b.mm(PSOU[:, 0:128], IDF[:, :], MADDA[:, :], False, True)
                    b.act(ESP[pp][:, :], PSOU[:, 0:128], AF.Exp, scale=-1.0,
                          bias=SCOLP[pp][:, 0:1])
                    b.mm(PSOU[:, 0:128], ONES1[0:1, :], SROW1P[pp][0:1, :], True,
                         False)
                    b.mm(PSOU[:, 0:128], IDF[:, :], MADDB[:, :], False, True)
                    b.act(ETIP[pp][:, :], PSOU[:, 0:128], AF.Exp, scale=1.0,
                          bias=NSCOLP[pp][:, 0:1])
                    b.stt(NMATP[pp][:, :], MKKK[kh % 2][:, :], NEGB4[:, vh:vh + 1], ESP[pp][:, :],
                          op0=ALU.mult, op1=ALU.mult)
                    b.tp(PS1B[:, 0:128], NMATP[pp][:, :], NTMATP[pp][:, :], ID[:])
                    b.ts(KSC_P[pp][:, :], ktm, BCOLP[pp][:, 0:1], op0=ALU.mult)
                    b.tp(PS1B[:, 0:128], KSC_P[pp][:, :], KSFP[pp][:, :], ID[:])
                    b.mm(PSM[:, 0:128], KSFP[pp][:, :],
                         SB[:, vh * 128:(vh + 1) * 128], True, True)
                    b.ts(VBETAP[pp][:, :], VB[:, vh * 128:(vh + 1) * 128],
                         BETA4[:, vh:vh + 1], op0=ALU.mult)
                    b.stt(UBP[pp][:, :], PSM[:, 0:128], NEGB4[:, vh:vh + 1],
                          VBETAP[pp][:, :], op0=ALU.mult, op1=ALU.add)
                    npow, ntpow = NMATP[pp], NTMATP[pp]
                    np2, ntp2 = NP2P[pp], NTP2P[pp]
                    for lvl in range(7):
                        b.mm(PSA[:, 0:128], ntpow[:, :], UBP[pp][:, :], True, False)
                        b.mm(PSA[:, 0:128], ID[:, :], UBP[pp][:, :], False, True)
                        b.cp(UBP[pp][:, :], PSA[:, 0:128])
                        if lvl < 6:
                            b.mm(PSA[:, 0:128], npow[:, :], ntpow[:, :], True,
                                 True)
                            b.cp(ntp2[:, :], PSA[:, 0:128])
                            if lvl < 5:
                                b.mm(PSA[:, 0:128], ntpow[:, :], npow[:, :],
                                     True, True)
                                b.cp(np2[:, :], PSA[:, 0:128])
                            npow, np2 = np2, npow
                            ntpow, ntp2 = ntp2, ntpow
                    b.tt(GTBP[pp][:, :], GRAWK[kh % 2][:, :], ETIP[pp][:, :], ALU.mult)
                    b.mm(PSM[:, 0:128], QFK[kh % 2][:, :],
                         SB[:, vh * 128:(vh + 1) * 128], True, True)
                    b.ts(OFFP[pp][:, :], PSM[:, 0:128], BCOLP[pp][:, 0:1], op0=ALU.mult)
                    b.mm(PSM2[:, 0:128], GTBP[pp][:, :], UBP[pp][:, :], True, True)
                    b.tt(OFFP[pp][:, :], OFFP[pp][:, :], PSM2[:, 0:128], ALU.add)
                    b.ts(KRBP[pp][:, :], ktm, RCOLP[pp][:, 0:1], op0=ALU.mult)
                    b.mm(PSM[:, 0:128], KRBP[pp][:, :], UBP[pp][:, :], True, True)
                    b.stt(SF[:, vh * 128:(vh + 1) * 128],
                          SF[:, vh * 128:(vh + 1) * 128], BCOLP[pp][:, 0:1],
                          PSM[:, 0:128], op0=ALU.mult, op1=ALU.add)
                    b.cp(SB[:, vh * 128:(vh + 1) * 128],
                         SF[:, vh * 128:(vh + 1) * 128])
                    b.act(VBETAP[pp][:, :], OFFP[pp][:, :], AF.Square,
                          accum_out=COLS[:, 0:1])
                    b.ts(COLS[:, 0:1], COLS[:, 0:1], 1.0 / DV)
                    b.ts(COLS[:, 0:1], COLS[:, 0:1], 1e-6, op0=ALU.add)
                    b.act(COLS[:, 1:2], COLS[:, 0:1], AF.Sqrt)
                    b.op("vector", "reciprocal", COLS[:, 2:3], COLS[:, 1:2])
                    b.ts(OFFP[pp][:, :], OFFP[pp][:, :], COLS[:, 2:3], op0=ALU.mult)
                    b.tt(OFFP[pp][:, :], OFFP[pp][:, :], NWB[:, :], ALU.mult)
                    b.tt(OGP[pp][:, :], OFFP[pp][:, :], ZTM[:, vh * 128:(vh + 1) * 128],
                         ALU.mult)
                    b.tp(PS1B[:, 0:128], OGP[pp][:, :], KSFP[pp][:, :], ID[:])
                    b.dma(obf_d[vh * 128:(vh + 1) * 128, rows], KSFP[pp][:, :])


def build_fused():
    nc = bass.Bass("TRN2", target_bir_lowering=False, debug=False)
    io = lambda n, s, dt, k="ExternalInput": nc.dram_tensor(
        n, s, dt, kind=k).ap()
    ins = {
        "x_in": io("x_in", [TS, D], F32),
        "pos": io("pos", [TS, 1], F32),
        "invf": io("invf", [1, 32], F32),
        "invf64": io("invf64", [64, 1], F32),
        "wqaT": io("wqaT", [D, Q_LORA], F32),
        "qlnw": io("qlnw", [1, Q_LORA], F32),
        "wqbTs": io("wqbTs", [128, 12 * 384], F32),
        "wkvaT": io("wkvaT", [D, KV_LORA + ROPE_D], F32),
        "kvlnw": io("kvlnw", [1, KV_LORA], F32),
        "qabsT": io("qabsT", [HPC * 128, 4 * NOPE], F32),
        "oaT": io("oaT", [HPC * 128, 4 * VH], F32),
        "woTs": io("woTs", [HPC * VH, D], F32),
        "fw": io("fw", [D, D], F32),
        "fwT": io("fwT", [D, D], F32),
        "fb": io("fb", [1, D], F32),
        "wouTs": io("wouTs", [VPC * DV, D], F32),
        "wq": io("wq", [D, 1544], F32),
        "convw": io("convw", [128, 32], F32),
        "alog": io("alog", [1, 4], F32),
        "dtb": io("dtb", [1, 4], F32),
        "normw": io("normw", [1, DV], F32),
    }
    x_out = io("x_out", [TS, D], F32, "ExternalOutput")

    dram = lambda n, s, dt, shared=False: nc.dram_tensor(
        n, s, dt, addr_space=("Shared" if shared else "Local"))
    fence_i = dram("fence_i", [16, 16], F32)
    fence_o = dram("fence_o", [NCORES * 16, 16], F32, True)
    agql_i = dram("agql_i", [Q_LORA, TS], BF16)
    agql_o = dram("agql_o", [NCORES * Q_LORA, TS], BF16, True)
    agkv_i = dram("agkv_i", [640, TS], BF16)
    agkv_o = dram("agkv_o", [NCORES * 640, TS], BF16, True)
    aglnx_i = dram("aglnx_i", [D, TS], BF16)
    aglnx_o = dram("aglnx_o", [NCORES * D, TS], BF16, True)
    xhf_d = dram("xhf_d", [HPC * VH, S], BF16)
    obf_d = dram("obf_d", [VPC * DV, S], BF16)
    rs1_i = dram("rs1_i", [S, D], BF16)
    rs1_o = dram("rs1_o", [TS, D], BF16)
    rs2_i = dram("rs2_i", [S, D], BF16)
    rs2_o = dram("rs2_o", [TS, D], BF16)
    x2_d = dram("x2_d", [TS, D], F32)

    with contextlib.ExitStack() as gstack:
        GE = gstack.enter_context
        ID = GE(nc.sbuf_tensor("ID", [128, 128], BF16))
        IDF = GE(nc.sbuf_tensor("IDF", [128, 128], F32))
        COLS = GE(nc.sbuf_tensor("COLS", [128, 8], F32))
        C1 = GE(nc.sbuf_tensor("C1", [128, 1], F32))
        sems = {d: GE(nc.semaphore(d)) for d in
                ("c_tensor", "c_vector", "c_scalar", "c_gpsimd", "c_sync",
                 "d_sync", "d_gpsimd", "d_scalar")}
        block = GE(nc.Block())
        G = {"ID": ID, "IDF": IDF, "COLS": COLS}
        b = B(nc)
        emit_identity(b, ID)
        emit_identity(b, IDF)
        b.op("vector", "memset", C1[:], 1.0)
        nc.const_aps.register(F32, 1.0, C1[:, 0:1].opt())

        _s = len(b.seq)
        def fence():
            b.coll("AllGather", ALU.bypass, fence_i, fence_o)

        emit_phase1(nc, b, G, ins, agql_i, agkv_i)
        b.mark_phase("p1", _s, len(b.seq))
        b.coll("AllGather", ALU.bypass, agql_i, agql_o)
        b.coll("AllGather", ALU.bypass, agkv_i, agkv_o)
        fence()
        b.barrier()
        _s = len(b.seq)
        emit_phase2(nc, b, G, ins, agql_o, agkv_o, xhf_d)
        b.mark_phase("p2", _s, len(b.seq))
        b.barrier()
        _s = len(b.seq)
        emit_partial_proj(nc, b, G, "2b", HPC, xhf_d, ins["woTs"], rs1_i)
        b.mark_phase("p2b", _s, len(b.seq))
        b.coll("ReduceScatter", ALU.add, rs1_i, rs1_o)
        fence()
        b.barrier()
        _s = len(b.seq)
        emit_resid_ffn(nc, b, G, "3", ins, rs1_o, ins["x_in"], x2_d, aglnx_i)
        b.mark_phase("p3", _s, len(b.seq))
        b.coll("AllGather", ALU.bypass, aglnx_i, aglnx_o)
        fence()
        b.barrier()
        _s = len(b.seq)
        emit_phase4(nc, b, G, ins, aglnx_o, obf_d)
        b.mark_phase("p4", _s, len(b.seq))
        b.barrier()
        _s = len(b.seq)
        emit_partial_proj(nc, b, G, "4b", VPC, obf_d, ins["wouTs"], rs2_i)
        b.mark_phase("p4b", _s, len(b.seq))
        b.coll("ReduceScatter", ALU.add, rs2_i, rs2_o)
        fence()
        b.barrier()
        _s = len(b.seq)
        emit_resid_ffn(nc, b, G, "5", ins, rs2_o, x2_d, x_out, None)
        b.mark_phase("p5", _s, len(b.seq))

        b.emit(block, sems)
    return nc


_ROPE_PERM = np.concatenate([np.arange(0, 64, 2), np.arange(1, 64, 2)])
_NC_CACHE = {}


def _stage_lhsT(A):
    return np.ascontiguousarray(
        A.T.reshape(4, 128, 128).transpose(1, 0, 2).reshape(128, 512))


def kernel(L0_attn_a_hidden_states, L0_attn_a_wq_a, L0_attn_a_q_a_ln_w,
           L0_attn_a_wq_b, L0_attn_a_wkv_a, L0_attn_a_kv_a_ln_w,
           L0_attn_a_wkv_b, L0_attn_a_wo, L0_attn_b_W_qkvz, L0_attn_b_W_ba,
           L0_attn_b_conv_weight, L0_attn_b_A_log, L0_attn_b_dt_bias,
           L0_attn_b_norm_weight, L0_attn_b_W_out, L0_ffn_weight,
           L0_ffn_bias):
    cores = list(range(NCORES))
    asc = np.ascontiguousarray
    x = asc(np.asarray(L0_attn_a_hidden_states)[0])

    perm = _ROPE_PERM
    wqb = np.asarray(L0_attn_a_wq_b)
    wkva = np.asarray(L0_attn_a_wkv_a)
    rows = list(range(KV_LORA)) + [KV_LORA + p for p in perm]
    wkvaT = asc(wkva[rows, :].T)
    wqaT = asc(np.asarray(L0_attn_a_wq_a).T)
    invf = (1.0 / 10000.0 ** (np.arange(0, ROPE_D, 2, dtype=np.float32)
                              / ROPE_D)).astype(np.float32)
    invf64 = asc(np.concatenate([invf, invf]).reshape(64, 1))
    qlnw = asc(np.asarray(L0_attn_a_q_a_ln_w).reshape(1, -1))
    kvlnw = asc(np.asarray(L0_attn_a_kv_a_ln_w).reshape(1, -1))
    w_kvb = np.asarray(L0_attn_a_wkv_b).reshape(NH, NOPE + VH, KV_LORA)
    woT = np.asarray(L0_attn_a_wo).T
    fw = asc(np.asarray(L0_ffn_weight))
    fwT = asc(fw.T)
    fb = asc(np.asarray(L0_ffn_bias).reshape(1, -1))
    wout = np.asarray(L0_attn_b_W_out)
    Wq = np.asarray(L0_attn_b_W_qkvz)
    Wba = np.asarray(L0_attn_b_W_ba)
    cw = np.asarray(L0_attn_b_conv_weight)
    alog = np.asarray(L0_attn_b_A_log)
    dtb = np.asarray(L0_attn_b_dt_bias)
    normw = asc(np.asarray(L0_attn_b_norm_weight).reshape(1, DV))

    if "fused" not in _NC_CACHE:
        _NC_CACHE["fused"] = build_fused()
    nc = _NC_CACHE["fused"]

    maps = []
    for c in cores:
        sl = slice(c * TS, (c + 1) * TS)
        h0, h1 = HPC * c, HPC * c + 1
        cols = (list(range(h0 * QH, h0 * QH + NOPE))
                + list(range(h1 * QH, h1 * QH + NOPE))
                + [h0 * QH + NOPE + p for p in perm[:32]]
                + [h1 * QH + NOPE + p for p in perm[:32]]
                + [h0 * QH + NOPE + p for p in perm[32:]]
                + [h1 * QH + NOPE + p for p in perm[32:]])
        wqbTs = asc(wqb[cols, :].T.reshape(12, 128, 384)
                    .transpose(1, 0, 2).reshape(128, 12 * 384))
        qabsT = np.concatenate(
            [_stage_lhsT(w_kvb[HPC * c + hh, :NOPE, :]) for hh in range(HPC)],
            0).astype(np.float32)
        oaT = np.concatenate(
            [_stage_lhsT(w_kvb[HPC * c + hh, VH:, :]) for hh in range(HPC)],
            0).astype(np.float32)
        woTs = asc(woT[c * HPC * VH:(c + 1) * HPC * VH, :])
        wouTs = asc(wout[c * VPC * DV:(c + 1) * VPC * DV, :])
        khs = [KPC * c, KPC * c + 1]
        qc = np.concatenate([np.arange(kh * 768, kh * 768 + 128)
                             for kh in khs])
        kc_ = np.concatenate([np.arange(kh * 768 + 128, kh * 768 + 256)
                              for kh in khs])
        vc = np.concatenate([np.arange(kh * 768 + 256, kh * 768 + 512)
                             for kh in khs])
        zc = np.concatenate([np.arange(kh * 768 + 512, kh * 768 + 768)
                             for kh in khs])
        bac = np.concatenate([np.arange(kh * 4, kh * 4 + 4) for kh in khs])
        wq_stage = asc(np.concatenate(
            [Wq[:, qc], Wq[:, kc_], Wq[:, vc], Wq[:, zc], Wba[:, bac]], 1))
        conv_rows = np.concatenate([
            np.arange(KPC * c * 128, (KPC * c + 2) * 128),
            2048 + np.arange(KPC * c * 128, (KPC * c + 2) * 128),
            4096 + np.arange(VPC * c * 128, (VPC * c + 4) * 128)])
        convw = asc(cw[conv_rows].reshape(8, 128, 4)
                    .transpose(1, 0, 2).reshape(128, 32))
        maps.append({
            "x_in": asc(x[sl]),
            "pos": np.arange(c * TS, (c + 1) * TS,
                             dtype=np.float32).reshape(-1, 1),
            "invf": asc(invf.reshape(1, 32)), "invf64": invf64,
            "wqaT": wqaT, "qlnw": qlnw, "wqbTs": wqbTs,
            "wkvaT": wkvaT, "kvlnw": kvlnw, "qabsT": qabsT, "oaT": oaT,
            "woTs": woTs, "fw": fw, "fwT": fwT, "fb": fb, "wouTs": wouTs,
            "wq": wq_stage, "convw": convw,
            "alog": asc(alog[VPC * c:VPC * c + 4].reshape(1, 4)),
            "dtb": asc(dtb[VPC * c:VPC * c + 4].reshape(1, 4)),
            "normw": normw})
    res = run_bass_kernel_spmd(nc, maps, core_ids=cores)
    x4 = np.concatenate([r["x_out"] for r in res.results], 0)
    return x4.reshape(1, S, D)
